# version 16
# baseline (speedup 1.0000x reference)
import numpy as np
import concourse.bass as bass
import concourse.mybir as mybir
from concourse.bass_utils import run_bass_kernel_spmd

F32 = mybir.dt.float32
BF16 = mybir.dt.bfloat16
I32 = mybir.dt.int32
AF = mybir.ActivationFunctionType
OP = mybir.AluOpType
AX = mybir.AxisListType

P = 128
D = 2048
E = 4096
HID = 5632
R = 128
NS = 16
KCONV = 4
SEQS = 2
L = 2048
T = 512
MT = SEQS * (L // T)
TSB = T // P
KD = D // P
NE = E // P
KH = HID // P
EPS = 1e-5
NCORES = 8
HH = HID // 2

DMA_SEMS = ["qw0", "qw1", "qd0", "qd1", "qz", "qo", "qx", "qb", "qp"]
ENG_SEMS = ["pe", "ev", "vv", "dv", "ac", "sc", "ad"]


def _rearr_col(v, nchunk):
    return np.ascontiguousarray(np.asarray(v).reshape(nchunk, P).T).astype(np.float32)


def _prep_inputs(inp):
    asc = np.ascontiguousarray
    h = {}
    h["inwT"] = asc(np.asarray(inp["L0_attn_in_w"].T, np.float32))
    h["skipwT"] = asc(np.asarray(inp["L0_attn_skip_w"].T, np.float32))
    h["outwT"] = asc(np.asarray(inp["L0_attn_out_w"].T, np.float32))
    h["wd1T"] = asc(np.asarray(inp["L0_attn_wd1"].T, np.float32))
    h["wd2T"] = asc(np.asarray(inp["L0_attn_wd2"].T, np.float32))
    h["wBT"] = asc(np.asarray(inp["L0_attn_wB"].T, np.float32))
    h["wCT"] = asc(np.asarray(inp["L0_attn_wC"].T, np.float32))
    h["gwT"] = asc(np.asarray(inp["L0_ffn_wq_gate"].T, np.int32))
    h["uwT"] = asc(np.asarray(inp["L0_ffn_wq_up"].T, np.int32))
    h["dwT"] = asc(np.asarray(inp["L0_ffn_wq_down"].T, np.int32))
    h["sgate_c"] = _rearr_col(inp["L0_ffn_s_gate"], KH)
    h["sup_c"] = _rearr_col(inp["L0_ffn_s_up"], KH)
    h["s_down"] = asc(np.asarray(inp["L0_ffn_s_down"], np.float32).reshape(1, D))
    h["normw"] = _rearr_col(inp["L0_attn_norm_w"], KD)
    h["convw"] = asc(np.asarray(inp["L0_attn_conv_w"], np.float32).reshape(NE, P, KCONV)
                     .transpose(1, 0, 2).reshape(P, NE * KCONV))
    h["convb"] = _rearr_col(inp["L0_attn_conv_b"], NE)
    h["wd2b"] = _rearr_col(inp["L0_attn_wd2_b"], NE)
    h["WD"] = _rearr_col(inp["L0_attn_W_D"], NE)
    h["Alog"] = asc(np.asarray(inp["L0_attn_A_log"], np.float32).reshape(NE, P, NS)
                    .transpose(1, 0, 2).reshape(P, NE * NS))
    return h


def build():
    nc = bass.Bass(target_bir_lowering=False)

    resid = nc.dram_tensor("resid", [SEQS, L, D], F32, kind="ExternalInput")
    inwT = nc.dram_tensor("inwT", [D, E], F32, kind="ExternalInput")
    skipwT = nc.dram_tensor("skipwT", [D, E], F32, kind="ExternalInput")
    outwT = nc.dram_tensor("outwT", [E, D], F32, kind="ExternalInput")
    wd1T = nc.dram_tensor("wd1T", [E, R], F32, kind="ExternalInput")
    wd2T = nc.dram_tensor("wd2T", [R, E], F32, kind="ExternalInput")
    wBT = nc.dram_tensor("wBT", [E, NS], F32, kind="ExternalInput")
    wCT = nc.dram_tensor("wCT", [E, NS], F32, kind="ExternalInput")
    gwT = nc.dram_tensor("gwT", [D, HID], I32, kind="ExternalInput")
    uwT = nc.dram_tensor("uwT", [D, HID], I32, kind="ExternalInput")
    dwT = nc.dram_tensor("dwT", [HID, D], I32, kind="ExternalInput")
    sgate_in = nc.dram_tensor("sgate_c", [P, KH], F32, kind="ExternalInput")
    sup_in = nc.dram_tensor("sup_c", [P, KH], F32, kind="ExternalInput")
    s_down = nc.dram_tensor("s_down", [1, D], F32, kind="ExternalInput")
    normw_in = nc.dram_tensor("normw", [P, KD], F32, kind="ExternalInput")
    convw_in = nc.dram_tensor("convw", [P, NE * KCONV], F32, kind="ExternalInput")
    convb_in = nc.dram_tensor("convb", [P, NE], F32, kind="ExternalInput")
    wd2b_in = nc.dram_tensor("wd2b", [P, NE], F32, kind="ExternalInput")
    WD_in = nc.dram_tensor("WD", [P, NE], F32, kind="ExternalInput")
    Alog_in = nc.dram_tensor("Alog", [P, NE * NS], F32, kind="ExternalInput")
    out = nc.dram_tensor("out", [SEQS, L, D], F32, kind="ExternalOutput")

    sc_in = nc.dram_tensor("sc_in", [NE, P, KD * P], BF16)
    sc_skip = nc.dram_tensor("sc_skip", [NE, P, KD * P], BF16)
    sc_out = nc.dram_tensor("sc_out", [NE, P, D], BF16)
    sc_g = nc.dram_tensor("sc_g", [KH, P, KD * P], BF16)
    sc_u = nc.dram_tensor("sc_u", [KH, P, KD * P], BF16)
    sc_d = nc.dram_tensor("sc_d", [KH, P, D], BF16)
    sc_bc = nc.dram_tensor("sc_bc", [2 * NS, T], BF16)
    sc_wd = nc.dram_tensor("sc_wd", [2, P, E], BF16)
    sc_x1 = nc.dram_tensor("sc_x1", [TSB, P, D], F32)

    from contextlib import ExitStack
    with ExitStack() as ctx:
        En = ctx.enter_context

        def sbt(name, shape, dt=BF16):
            return En(nc.sbuf_tensor(name, shape, dt))

        consts = sbt("consts", [P, 3232])
        ident = consts[:, 0:256].bitcast(F32)
        A_sb = consts[:, 256:1280].bitcast(F32)
        convw_sb = consts[:, 1280:1536].bitcast(F32)
        convb_sb = consts[:, 1536:1600].bitcast(F32)
        wd2b_sb = consts[:, 1600:1664].bitcast(F32)
        WD_sb = consts[:, 1664:1728].bitcast(F32)
        normw_sb = consts[:, 1728:1760].bitcast(F32)
        cols = consts[:, 1760:1792].bitcast(F32)
        cst = consts[:, 1792:1808].bitcast(F32)
        wB_sb = consts[:, 1808:2320]
        wC_sb = consts[:, 2320:2832]
        ones1 = consts[0:1, 2832:2960]
        tails = consts[:, 2960:3056]
        sgh_sb = consts[:, 3056:3144].bitcast(F32)
        su_sb = consts[:, 3144:3232].bitcast(F32)

        x_fm = sbt("x_fm", [P, NE * T])
        dy_fm = sbt("dy_fm", [P, NE * T])
        z_x = x_fm[:, 0:4096].bitcast(F32)
        rn_fm = sbt("rn_fm", [P, KD * T])
        rn_tm = sbt("rn_tm", [P, D], F32)
        wbuf = sbt("wbuf", [P, 2, KD * P])
        dbuf = sbt("dbuf", [P, 2, D])

        arena = sbt("arena", [P, 49728])
        Brep = arena[:, 0:NS * T]
        Crep = arena[:, NS * T:2 * NS * T]
        h_all = arena[:, 16384:16384 + NS * T]
        d_buf = arena[:, 25088:25088 + 2 * T]
        carry = arena[:, 26112:26112 + 2 * NE * NS].bitcast(F32)
        tap = arena[:, 27136:27136 + 4 * T]
        brow = arena[0:1, 27136:27136 + 4 * T]
        bc_nt = arena[0:2 * NS, 29184:29184 + T]
        bc_t_f = arena[:, 29696:29760].bitcast(F32)
        gt_b = arena[:, 29760:29760 + T // 2]
        up_b = arena[:, 30016:30016 + T // 2]
        x1_fm = arena[:, 30272:30272 + KD * T]
        gu_half = arena[:, 38464:38464 + KH * (T // 2)]
        z_ln2 = arena[:, 38464 + 7168:38464 + 11264].bitcast(F32)
        rms_f = arena[:, 38464:38464 + 2 * D].bitcast(F32)
        conv_f = arena[:, 42560:43584].bitcast(F32)
        xin_buf = arena[:, 43584:43584 + KCONV - 1 + T]
        t1_sb = arena[:, 44168:44168 + T]
        pbufA = arena[:, 0:8192].bitcast(F32)
        pbufB = arena[:, 8192:16384].bitcast(F32)
        pout16A = arena[:, 16384:21760]
        pout16B = arena[:, 21760:27136]
        ubufA = arena[:, 30272:35904]
        ubufB = arena[:, 35904:41536]
        uoutA = arena[:, 41536:44352]
        uoutB = arena[:, 44352:47168]
        srepD_sb = arena[:, 47168:49216]

        ps = [En(nc.psum_tensor(f"ps{i}", [P, 512], F32)) for i in range(8)]

        sems = {}
        for name in DMA_SEMS + ENG_SEMS:
            sems[name] = En(nc.semaphore(name))

        block = En(nc.Block())
        plan = {k: [] for k in ["sync", "gpsimd", "tensor", "vector", "scalar"]}
        n = {k: 0 for k in sems}
        cur_scope = [None]

        def emit(eng, fn):
            plan[eng].append((cur_scope[0], fn))

        def wait(eng, sem, val):
            if val is None or val <= 0:
                return
            emit(eng, lambda e, sem=sem, val=val: e.wait_ge(sems[sem], val))

        def dma(eng, sem, dst, src):
            n[sem] += 16
            v = n[sem]
            emit(eng, lambda e, dst=dst, src=src, sem=sem:
                 e.dma_start(out=dst, in_=src).then_inc(sems[sem], 16))
            return v

        def op(eng, fn, sem=None):
            if sem is None:
                emit(eng, fn)
                return None
            n[sem] += 1
            v = n[sem]
            emit(eng, lambda e, fn=fn, sem=sem: fn(e).then_inc(sems[sem], 1))
            return v

        bank_free = {i: ("ev", 0) for i in range(8)}

        def _evs(v):
            return ("ev", v)

        def _vvs(v):
            return ("vv", v)

        def mm_group(psb, tiles, extra_wait=None):
            first = True
            nt = len(tiles)
            res = None
            for j, (lf, rf, of) in enumerate(tiles):
                if first:
                    wait("tensor", bank_free[psb][0], bank_free[psb][1])
                    if extra_wait is not None:
                        wait("tensor", extra_wait[0], extra_wait[1])
                    first = False
                st, sp = (j == 0), (j == nt - 1)
                f = (lambda e, lf=lf, rf=rf, of=of, st=st, sp=sp:
                     e.matmul(of(), lf(), rf(), start=st, stop=sp))
                if sp:
                    res = op("tensor", f, "pe")
                else:
                    op("tensor", f)
            return res

        ev_ms = op("gpsimd", lambda e: e.memset(ident[:], 0.0), "sc")
        wait("gpsimd", "sc", ev_ms)
        ev_ident = op("gpsimd", lambda e: e.affine_select(
            out=ident[:], in_=ident[:], compare_op=OP.not_equal, fill=1.0,
            base=0, pattern=[[-1, P]], channel_multiplier=1), "sc")
        ev_ones = op("gpsimd", lambda e: e.memset(ones1[:], 1.0), "sc")
        op("gpsimd", lambda e: e.memset(cst[:, 0:1], -128.0))
        op("gpsimd", lambda e: e.memset(cst[:, 1:2], EPS))
        op("gpsimd", lambda e: e.memset(cst[:, 2:3], -0.5))
        op("gpsimd", lambda e: e.memset(cst[:, 3:4], 1.0 / D))
        ev_cst = op("gpsimd", lambda e: e.memset(cst[:, 4:5], 0.5), "sc")

        dma("sync", "qw0", normw_sb[:], normw_in[:])
        dma("sync", "qw0", convw_sb[:], convw_in[:])
        dma("sync", "qw0", convb_sb[:], convb_in[:])
        dma("sync", "qw0", wd2b_sb[:], wd2b_in[:])
        dma("sync", "qw0", WD_sb[:], WD_in[:])
        dma("sync", "qw0", sgh_sb[:], sgate_in[:])
        dma("sync", "qw0", su_sb[:], sup_in[:])
        v = dma("sync", "qw0", A_sb[:], Alog_in[:])
        wait("sync", "qw0", v)
        wait("scalar", "qw0", v)
        ev_aexp = op("scalar", lambda e: e.activation(
            out=A_sb[:], in_=A_sb[:], func=AF.Exp), "ac")
        wait("vector", "ac", ev_aexp)
        ev_A = op("vector", lambda e: e.tensor_scalar(
            out=A_sb[:], in0=A_sb[:], scalar1=-1.0, scalar2=None, op0=OP.mult), "dv")
        ev_sgh = op("vector", lambda e: e.tensor_scalar(
            out=sgh_sb[:], in0=sgh_sb[:], scalar1=0.5, scalar2=None, op0=OP.mult), "dv")

        for c in range(NE):
            dma("sync", "qw0", pbufA[:, c * R:(c + 1) * R], wd1T[c * P:(c + 1) * P, :])
        wait("sync", "qw0", n["qw0"])
        wait("scalar", "qw0", n["qw0"])
        e1 = op("scalar", lambda e: e.activation(
            out=pout16A[:, 0:NE * R], in_=pbufA[:, 0:NE * R], func=AF.Identity), "ac")
        wait("gpsimd", "ac", e1)
        dma("gpsimd", "qp", sc_wd[0, :, 0:NE * R], pout16A[:, 0:NE * R])
        v = dma("sync", "qw0", pbufB[:, 0:E], wd2T[:])
        wait("sync", "qw0", v)
        wait("scalar", "qw0", v)
        e2 = op("scalar", lambda e: e.activation(
            out=pout16B[:, 0:E], in_=pbufB[:, 0:E], func=AF.Identity), "ac")
        wait("gpsimd", "ac", e2)
        dma("gpsimd", "qb", sc_wd[1], pout16B[:, 0:E])
        wait("sync", "ac", e1)
        for c in range(NE):
            dma("sync", "qw0", pbufA[:, c * NS:(c + 1) * NS], wBT[c * P:(c + 1) * P, :])
        wait("vector", "qw0", n["qw0"])
        ebc1 = op("vector", lambda e: e.tensor_copy(
            out=wB_sb[:], in_=pbufA[:, 0:NE * NS]), "dv")
        wait("sync", "dv", ebc1)
        for c in range(NE):
            dma("sync", "qw0", pbufA[:, c * NS:(c + 1) * NS], wCT[c * P:(c + 1) * P, :])
        wait("vector", "qw0", n["qw0"])
        ev_smallw = op("vector", lambda e: e.tensor_copy(
            out=wC_sb[:], in_=pbufA[:, 0:NE * NS]), "dv")
        wait("sync", "dv", ev_smallw)

        last_cast = {0: e1, 1: e2}
        last_store = {0: n["qp"], 1: n["qb"]}

        def cast_tile_w1(src_rows, scratch, kd, par):
            qs = "qw0" if par == 0 else "qw1"
            ss = "qp" if par == 0 else "qb"
            pb = pbufA if par == 0 else pbufB
            po = pout16A if par == 0 else pout16B
            wait("sync", "ac", last_cast[par])
            v = dma("sync", qs, pb[:, 0:E], src_rows)
            wait("scalar", qs, v)
            wait("scalar", ss, last_store[par])
            ec = op("scalar", lambda e, pb=pb, po=po: e.activation(
                out=po[:, 0:E], in_=pb[:, 0:E], func=AF.Identity), "ac")
            last_cast[par] = ec
            wait("gpsimd", "ac", ec)
            for g in range(4):
                dma("gpsimd", ss,
                    scratch[g * 8:(g + 1) * 8, :, kd * P:(kd + 1) * P]
                    .rearrange("c p j -> p c j"),
                    po[:, g * 8 * P:(g + 1) * 8 * P].rearrange("p (c j) -> p c j", j=P))
            last_store[par] = n[ss]

        ti = 0
        for scratch, src in [(sc_in, inwT), (sc_skip, skipwT)]:
            for kd in range(KD):
                cast_tile_w1(src[kd * P:(kd + 1) * P, :], scratch, kd, ti % 2)
                ti += 1
        IN_SKIP_QP = n["qp"]
        IN_SKIP_QB = n["qb"]

        wait("gpsimd", "ac", max(last_cast[0], last_cast[1]))
        op("gpsimd", lambda e: e.memset(carry[:], 0.0))
        ev_zero = op("gpsimd", lambda e: e.memset(tails[:], 0.0), "sc")

        marks = {}
        ulast_cast = {0: 0, 1: 0}
        ulast_store = {0: 0, 1: 0}
        upar = [0]
        ust = {"srepd_pe": 0, "srepd_sc": 0}

        def u_srepd():
            prow_f = arena[0:1, 30272:34368].bitcast(F32)
            prow16 = uoutA[0:1, 0:D]
            wait("sync", "qw0", n["qw0"])
            v = dma("sync", "qw0", prow_f[:], s_down[:])
            wait("scalar", "qw0", v)
            ec = op("scalar", lambda e: e.activation(
                out=prow16[:], in_=prow_f[:], func=AF.Identity), "ac")
            ulast_cast[0] = ec
            wait("tensor", "ac", ec)
            wait("tensor", "sc", ev_ones)
            for b_ in range(4):
                wait("tensor", bank_free[7][0], bank_free[7][1])
                pv = op("tensor", lambda e, b_=b_: e.matmul(
                    ps[7][:], ones1[0:1, :], prow16[0:1, b_ * 512:(b_ + 1) * 512],
                    start=True, stop=True), "pe")
                ust["srepd_pe"] = pv
                wait("scalar", "pe", pv)
                bank_free[7] = _evs(op("scalar", lambda e, b_=b_: e.activation(
                    out=srepD_sb[:, b_ * 512:(b_ + 1) * 512], in_=ps[7][:],
                    func=AF.Identity), "ev"))
            ust["srepd_sc"] = n["ev"]

        def u_out(c):
            par = upar[0]
            upar[0] ^= 1
            qs = "qw0" if par == 0 else "qw1"
            ss = "qp" if par == 0 else "qb"
            ub = (ubufA if par == 0 else ubufB).bitcast(F32)
            uo = uoutA if par == 0 else uoutB
            wait("sync", "ac", ulast_cast[par])
            if c == 0:
                wait("sync", "pe", ust["srepd_pe"])
            v = dma("sync", qs, ub[:, 0:D], outwT[c * P:(c + 1) * P, :])
            wait("scalar", qs, v)
            wait("scalar", ss, ulast_store[par])
            if c == 0:
                wait("scalar", "ev", ust["srepd_sc"])
            ec = op("scalar", lambda e, ub=ub, uo=uo: e.activation(
                out=uo[:, 0:D], in_=ub[:, 0:D], func=AF.Identity), "ac")
            ulast_cast[par] = ec
            wait("gpsimd", "ac", ec)
            dma("gpsimd", ss, sc_out[c], uo[:, 0:D])
            ulast_store[par] = n[ss]
            if c == NE - 1:
                marks["out_qp"] = n["qp"]
                marks["out_qb"] = n["qb"]

        def u_q(src, scratch, kd, hh):
            par = upar[0]
            upar[0] ^= 1
            qs = "qw0" if par == 0 else "qw1"
            ss = "qp" if par == 0 else "qb"
            ub = (ubufA if par == 0 else ubufB).bitcast(I32)
            uo = uoutA if par == 0 else uoutB
            wait("sync", "ac", ulast_cast[par])
            v = dma("sync", qs, ub[:, 0:HH], src[kd * P:(kd + 1) * P, hh * HH:(hh + 1) * HH])
            wait("scalar", qs, v)
            wait("scalar", ss, ulast_store[par])
            ec = op("scalar", lambda e, ub=ub, uo=uo: e.activation(
                out=uo[:, 0:HH], in_=ub[:, 0:HH], func=AF.Identity,
                bias=cst[:, 0:1]), "ac")
            ulast_cast[par] = ec
            wait("gpsimd", "ac", ec)
            m0 = hh * (KH // 2)
            for g in range(2):
                dma("gpsimd", ss,
                    scratch[m0 + g * 11:m0 + (g + 1) * 11, :, kd * P:(kd + 1) * P]
                    .rearrange("m p j -> p m j"),
                    uo[:, g * 11 * P:(g + 1) * 11 * P].rearrange("p (m j) -> p m j", j=P))
            ulast_store[par] = n[ss]

        def u_down(kh):
            par = upar[0]
            upar[0] ^= 1
            qs = "qw0" if par == 0 else "qw1"
            ss = "qp" if par == 0 else "qb"
            ub = (ubufA if par == 0 else ubufB).bitcast(I32)
            uo = uoutA if par == 0 else uoutB
            wait("sync", "ac", ulast_cast[par])
            v = dma("sync", qs, ub[:, 0:D], dwT[kh * P:(kh + 1) * P, :])
            wait("scalar", qs, v)
            wait("scalar", ss, ulast_store[par])
            ec = op("scalar", lambda e, ub=ub, uo=uo: e.activation(
                out=uo[:, 0:D], in_=ub[:, 0:D], func=AF.Identity,
                bias=cst[:, 0:1]), "ac")
            ulast_cast[par] = ec
            wait("gpsimd", "ac", ec)
            if kh == 0:
                wait("gpsimd", "ev", ust["srepd_sc"])
            gv = op("gpsimd", lambda e, uo=uo: e.tensor_tensor(
                out=uo[:, 0:D], in0=uo[:, 0:D], in1=srepD_sb[:], op=OP.mult), "sc")
            ulast_cast[par] = max(ulast_cast[par], ec)
            wait("gpsimd", "sc", gv)
            dma("gpsimd", ss, sc_d[kh], uo[:, 0:D])
            ulast_store[par] = n[ss]
            if kh == KH - 1:
                marks["ffn_qp"] = n["qp"]
                marks["ffn_qb"] = n["qb"]

        def build_prep_units():
            units = [u_srepd]
            for c in range(NE):
                units.append(lambda c=c: u_out(c))
            for src, scratch in [(gwT, sc_g), (uwT, sc_u)]:
                for kd in range(KD):
                    for hh in range(2):
                        units.append(lambda src=src, scratch=scratch, kd=kd, hh=hh:
                                     u_q(src, scratch, kd, hh))
            for kh in range(KH):
                units.append(lambda kh=kh: u_down(kh))
            return units

        guard_w = {0: 0, 1: 0}
        guard_d = {0: 0, 1: 0}
        scan_ev = []
        st_ = {"lastdma_rn_pe": 0, "x1_use_pe": 0, "prev_gu": 0}
        TH = T // 2

        def seq_of(mt):
            return mt // (L // T), (mt % (L // T)) * T

        def emit_ph1(mt):
            cur_scope[0] = f"m{mt}_ph1"
            sq, tok0 = seq_of(mt)
            e3_prev = None
            for ts in range(TSB):
                wait("gpsimd", "dv", n["dv"])
                va = dma("gpsimd", "qz", z_x[:],
                         resid[sq, tok0 + ts * P: tok0 + (ts + 1) * P, :])
                wait("scalar", "qz", va)
                if e3_prev:
                    wait("scalar", "dv", e3_prev)
                if mt == 0 and ts == 0:
                    wait("scalar", "sc", ev_cst)
                a1 = op("scalar", lambda e: e.activation(
                    out=rms_f[:], in_=z_x[:], func=AF.Square, accum_out=cols[:, 0:1]), "ac")
                wait("scalar", "ac", a1)
                a2 = op("scalar", lambda e: e.activation(
                    out=cols[:, 1:2], in_=cols[:, 0:1], func=AF.Ln,
                    scale=cst[:, 3:4], bias=cst[:, 1:2]), "ac")
                wait("scalar", "ac", a2)
                e2 = op("scalar", lambda e: e.activation(
                    out=cols[:, 2:3], in_=cols[:, 1:2], func=AF.Exp, scale=cst[:, 2:3]), "ac")
                wait("vector", "ac", e2)
                e3 = op("vector", lambda e: e.tensor_scalar(
                    out=rms_f[:], in0=z_x[:], scalar1=cols[:, 2:3], scalar2=None,
                    op0=OP.mult), "dv")
                e3_prev = e3
                wait("tensor", "dv", e3)
                if mt == 0 and ts == 0:
                    wait("tensor", "sc", ev_ident)
                for kd in range(KD):
                    psb = 2 + kd % 2
                    wait("tensor", bank_free[psb][0], bank_free[psb][1])
                    pv = op("tensor", lambda e, kd=kd, psb=psb: e.transpose(
                        ps[psb][:, 0:P], rms_f[:, kd * P:(kd + 1) * P], ident[:]), "pe")
                    wait("scalar", "pe", pv)
                    bank_free[psb] = _evs(op("scalar", lambda e, kd=kd, ts=ts, psb=psb:
                                             e.activation(
                        out=rn_fm[:, kd * T + ts * P: kd * T + (ts + 1) * P],
                        in_=ps[psb][:, 0:P], func=AF.Identity,
                        scale=normw_sb[:, kd:kd + 1]), "ev"))
            return n["ev"]

        def emit_ph2(mt, reset_ev, rn_ev):
            cur_scope[0] = f"m{mt}_ph2"
            if reset_ev:
                wait("vector", "sc", reset_ev)
            if mt == 0:
                wait("sync", "qp", IN_SKIP_QP)
                wait("sync", "qb", IN_SKIP_QB)
            prev_sil = None
            guard_p = {0: max(guard_w[0], guard_w[1]), 1: max(guard_d[0], guard_d[1])}
            for c in range(NE):
                par = c % 2
                if par == 0:
                    pi = (c // 2) % 2
                    buf = wbuf if pi == 0 else dbuf
                    qsem = "qw0" if pi == 0 else "qd0"
                    wait("sync", "pe", guard_p[pi])
                    vw = dma("sync", qsem, buf[:],
                             sc_in[c:c + 2].rearrange("c p d -> p c d"))
                    cur_vw = (qsem, vw)
                    cur_buf = buf
                    cur_pi = pi
                if par == 0:
                    wait("tensor", cur_vw[0], cur_vw[1])
                psb = par
                pev = mm_group(psb, [
                    (lambda kd=kd, par=par, cur_buf=cur_buf:
                     cur_buf[:, par, kd * P:(kd + 1) * P],
                     lambda kd=kd: rn_fm[:, kd * T:(kd + 1) * T],
                     lambda psb=psb: ps[psb][:]) for kd in range(KD)],
                    extra_wait=("ev", rn_ev) if c < 2 else None)
                guard_p[cur_pi] = pev
                wait("scalar", "pe", pev)
                if prev_sil:
                    wait("scalar", "dv", prev_sil)
                eev = op("scalar", lambda e, psb=psb: e.activation(
                    out=xin_buf[:, KCONV - 1:KCONV - 1 + T], in_=ps[psb][:],
                    func=AF.Identity), "ev")
                bank_free[psb] = _evs(eev)
                wait("vector", "ev", eev)
                v1 = op("vector", lambda e, c=c: e.tensor_copy(
                    out=xin_buf[:, 0:KCONV - 1],
                    in_=tails[:, c * (KCONV - 1):(c + 1) * (KCONV - 1)]), "dv")
                wait("vector", "dv", v1)
                op("vector", lambda e, c=c: e.tensor_copy(
                    out=tails[:, c * (KCONV - 1):(c + 1) * (KCONV - 1)],
                    in_=xin_buf[:, T:T + KCONV - 1]))
                acc = tap[:, 3 * T:4 * T]
                pv_ = op("vector", lambda e, c=c, acc=acc: e.tensor_scalar(
                    out=acc, in0=xin_buf[:, 0:T],
                    scalar1=convw_sb[:, c * KCONV:c * KCONV + 1],
                    scalar2=None, op0=OP.mult), "dv")
                for k in range(1, KCONV):
                    wait("vector", "dv", pv_)
                    pv_ = op("vector", lambda e, c=c, k=k, acc=acc: e.scalar_tensor_tensor(
                        out=acc, in0=xin_buf[:, k:k + T],
                        scalar=convw_sb[:, c * KCONV + k:c * KCONV + k + 1],
                        in1=acc, op0=OP.mult, op1=OP.add), "dv")
                edv = pv_
                prev_sil = edv
                wait("scalar", "dv", edv)
                op("scalar", lambda e, c=c, acc=acc: e.activation(
                    out=x_fm[:, c * T:(c + 1) * T], in_=acc, func=AF.Silu,
                    bias=convb_sb[:, c:c + 1]), "ac")
            guard_w[0] = guard_w[1] = guard_p[0]
            guard_d[0] = guard_d[1] = guard_p[1]

        def emit_ph34(mt):
            cur_scope[0] = f"m{mt}_ph34"
            wdv = wbuf[:].rearrange("p a b -> p (a b)")
            wait("sync", "pe", guard_w[0])
            wait("sync", "pe", guard_w[1])
            vw1 = dma("sync", "qw0", wdv[:, 0:NE * P], sc_wd[0, :, 0:NE * P])
            wait("tensor", "qw0", vw1)
            X_AC = n["ac"]
            pev = mm_group(0, [
                (lambda c=c: wdv[:, c * P:(c + 1) * P],
                 lambda c=c: x_fm[:, c * T:(c + 1) * T],
                 lambda: ps[0][:]) for c in range(NE)],
                extra_wait=("ac", X_AC))
            guard_w[0] = pev
            guard_w[1] = pev
            wait("scalar", "pe", pev)
            e_t1 = op("scalar", lambda e: e.activation(
                out=t1_sb, in_=ps[0][:], func=AF.Identity), "ev")
            bank_free[0] = _evs(e_t1)
            wait("sync", "pe", pev)
            vw2 = dma("sync", "qw0", wdv[:, 0:E], sc_wd[1, :, 0:E])
            wait("tensor", "qw0", vw2)
            prev_ln = None
            for c in range(NE):
                psb = c % 2
                wait("tensor", bank_free[psb][0], bank_free[psb][1])
                if c == 0:
                    wait("tensor", "ev", e_t1)
                pev = op("tensor", lambda e, c=c, psb=psb: e.matmul(
                    ps[psb][:], wdv[:, c * P:(c + 1) * P], t1_sb,
                    start=True, stop=True), "pe")
                wait("scalar", "pe", pev)
                if prev_ln:
                    wait("scalar", "ev", prev_ln)
                a4 = op("scalar", lambda e, c=c, psb=psb: e.activation(
                    out=conv_f[:, 0:T], in_=ps[psb][:], func=AF.Exp,
                    bias=wd2b_sb[:, c:c + 1]), "ac")
                wait("scalar", "ac", a4)
                lv = op("scalar", lambda e, c=c: e.activation(
                    out=dy_fm[:, c * T:(c + 1) * T], in_=conv_f[:, 0:T],
                    func=AF.Ln, bias=1.0), "ev")
                bank_free[psb] = _evs(lv)
                prev_ln = lv
            guard_w[0] = n["pe"]
            guard_w[1] = n["pe"]

        def emit_ph5(mt):
            cur_scope[0] = f"m{mt}_ph5"
            X_AC = n["ac"]
            if mt == 0:
                wait("tensor", "dv", ev_smallw)
            for ts in range(TSB):
                for wi, (wsb, psb, off) in enumerate([(wB_sb, 2, 0), (wC_sb, 3, NS)]):
                    pev = mm_group(psb, [
                        (lambda c=c, ts=ts: x_fm[:, c * T + ts * P: c * T + (ts + 1) * P],
                         lambda c=c, wsb=wsb: wsb[:, c * NS:(c + 1) * NS],
                         lambda psb=psb: ps[psb][:, 0:NS]) for c in range(NE)],
                        extra_wait=("ac", X_AC) if ts == 0 and wi == 0 else None)
                    wait("scalar", "pe", pev)
                    bank_free[psb] = _evs(op("scalar", lambda e, psb=psb, off=off:
                                             e.activation(
                        out=bc_t_f[:, off:off + NS], in_=ps[psb][:, 0:NS],
                        func=AF.Identity), "ev"))
                wait("tensor", "ev", n["ev"])
                wait("tensor", bank_free[2][0], bank_free[2][1])
                pv = op("tensor", lambda e: e.transpose(
                    ps[2][0:2 * NS, 0:P], bc_t_f[:], ident[:]), "pe")
                wait("scalar", "pe", pv)
                bank_free[2] = _evs(op("scalar", lambda e, ts=ts: e.activation(
                    out=bc_nt[:, ts * P:(ts + 1) * P], in_=ps[2][0:2 * NS, 0:P],
                    func=AF.Identity), "ev"))
            wait("gpsimd", "ev", n["ev"])
            wait("gpsimd", "qb", n["qb"])
            dma("gpsimd", "qb", sc_bc[:], bc_nt[:])
            wait("gpsimd", "qb", n["qb"])
            BCDST = [Brep, Crep]
            wait("tensor", "sc", ev_ones)
            for g in range(8):
                dma("gpsimd", "qb", brow[:, 0:4 * T],
                    sc_bc[:].rearrange("a b -> (a b)")[None, g * 4 * T:(g + 1) * 4 * T])
                wait("tensor", "qb", n["qb"])
                for j in range(4):
                    nidx = (g * 4 + j) % NS
                    dst = BCDST[(g * 4 + j) // NS]
                    psb = 4 + (j % 2)
                    wait("tensor", bank_free[psb][0], bank_free[psb][1])
                    pv = op("tensor", lambda e, j=j, psb=psb: e.matmul(
                        ps[psb][:], ones1[0:1, :], brow[:, j * T:(j + 1) * T],
                        start=True, stop=True), "pe")
                    wait("scalar", "pe", pv)
                    bank_free[psb] = _evs(op("scalar", lambda e, nidx=nidx, dst=dst,
                                             psb=psb: e.activation(
                        out=dst[:, nidx * T:(nidx + 1) * T],
                        in_=ps[psb][:], func=AF.Identity), "ev"))
                if g < 7:
                    wait("gpsimd", "pe", n["pe"])

        def build_ffn_units(mt):
            sq, tok0 = seq_of(mt)
            units = []
            gu_ev = {}

            pend = {}

            def gu_issue(th, kh):
                parw = (th * KH + kh) % 2
                rot = kh % 2
                bg, bu = (0, 1) if rot == 0 else (2, 3)
                qsw = f"qw{parw}"
                qsd = f"qd{parw}"
                wait("sync", "pe", guard_w[parw])
                vg = dma("sync", qsw, wbuf[:, parw], sc_g[kh])
                wait("sync", "pe", guard_d[parw])
                vu = dma("sync", qsd, dbuf[:, parw], sc_u[kh])
                wait("tensor", qsw, vg)
                gev = mm_group(bg, [
                    (lambda kd=kd, parw=parw: wbuf[:, parw, kd * P:(kd + 1) * P],
                     lambda kd=kd, th=th: x1_fm[:, kd * T + th * TH: kd * T + (th + 1) * TH],
                     lambda bg=bg: ps[bg][:, 0:TH]) for kd in range(KD)])
                guard_w[parw] = gev
                wait("tensor", qsd, vu)
                uev = mm_group(bu, [
                    (lambda kd=kd, parw=parw: dbuf[:, parw, kd * P:(kd + 1) * P],
                     lambda kd=kd, th=th: x1_fm[:, kd * T + th * TH: kd * T + (th + 1) * TH],
                     lambda bu=bu: ps[bu][:, 0:TH]) for kd in range(KD)])
                guard_d[parw] = uev
                st_["x1_use_pe"] = uev
                pend[(th, kh)] = (bg, bu, gev, uev)

            def gu_drain(th, kh):
                bg, bu, gev, uev = pend.pop((th, kh))
                par = kh % 2
                gb = tap[:, (2 + par) * TH:(3 + par) * TH]
                ub = tap[:, (4 + par) * TH:(5 + par) * TH]
                tb = tap[:, (6 + par) * TH:(7 + par) * TH]
                wait("scalar", "pe", gev)
                if st_["prev_gu"]:
                    wait("scalar", "dv", st_["prev_gu"])
                a_t = op("scalar", lambda e, bg=bg, tb=tb, kh=kh: e.activation(
                    out=tb, in_=ps[bg][:, 0:TH], func=AF.Tanh,
                    scale=sgh_sb[:, kh:kh + 1]), "ac")
                eg = op("scalar", lambda e, bg=bg, gb=gb, kh=kh: e.activation(
                    out=gb, in_=ps[bg][:, 0:TH], func=AF.Identity,
                    scale=sgh_sb[:, kh:kh + 1]), "ac")
                bank_free[bg] = ("ac", eg)
                wait("scalar", "pe", uev)
                eu = op("scalar", lambda e, bu=bu, ub=ub, kh=kh: e.activation(
                    out=ub, in_=ps[bu][:, 0:TH], func=AF.Identity,
                    scale=su_sb[:, kh:kh + 1]), "ac")
                bank_free[bu] = ("ac", eu)
                wait("vector", "ac", eu)
                m1 = op("vector", lambda e, gb=gb, ub=ub: e.tensor_tensor(
                    out=ub, in0=ub, in1=gb, op=OP.mult), "vv")
                wait("vector", "vv", m1)
                g_ = op("vector", lambda e, kh=kh, tb=tb, ub=ub: e.scalar_tensor_tensor(
                    out=gu_half[:, kh * TH:(kh + 1) * TH], in0=tb, scalar=1.0, in1=ub,
                    op0=OP.add, op1=OP.mult), "dv")
                st_["prev_gu"] = g_
                gu_ev[(th, kh)] = g_

            def down_unit(th, kh):
                parw = kh % 2
                qsd = f"qd{parw}"
                wait("sync", "pe", guard_d[parw])
                vw = dma("sync", qsd, dbuf[:, parw], sc_d[kh])
                wait("tensor", qsd, vw)
                wait("tensor", "dv", gu_ev[(th, kh)])
                for j in range(2):
                    for dg in range(4):
                        psb = j * 4 + dg
                        stt, sp = (kh == 0), (kh == KH - 1)
                        if stt:
                            wait("tensor", bank_free[psb][0], bank_free[psb][1])
                        f = (lambda e, kh=kh, parw=parw, j=j, dg=dg, psb=psb,
                             stt=stt, sp=sp:
                             e.matmul(ps[psb][:],
                                      gu_half[:, kh * TH + j * P: kh * TH + (j + 1) * P],
                                      dbuf[:, parw, dg * 512:(dg + 1) * 512],
                                      start=stt, stop=sp))
                        if j == 1 and dg == 3:
                            op("tensor", f, "pe")
                        else:
                            op("tensor", f)
                guard_d[parw] = n["pe"]
                st_["down_pe"] = n["pe"]

            def ln2_unit(th, jt):
                tsl = 2 * th + jt
                DOWN_PE = st_["down_pe"]
                wait("gpsimd", "dv", n["dv"])
                wait("gpsimd", "qx", n["qx"])
                wait("gpsimd", "pe", DOWN_PE)
                va = dma("gpsimd", "qx", z_ln2[:], sc_x1[tsl])
                wait("vector", "qx", va)
                wait("vector", "pe", DOWN_PE)
                for dg in range(4):
                    psb = jt * 4 + dg
                    bank_free[psb] = _vvs(op("vector", lambda e, psb=psb, dg=dg:
                                             e.tensor_tensor(
                        out=z_ln2[:, dg * 512:(dg + 1) * 512],
                        in0=z_ln2[:, dg * 512:(dg + 1) * 512],
                        in1=ps[psb][:], op=OP.add), "vv"))
                wait("scalar", bank_free[jt * 4 + 3][0], bank_free[jt * 4 + 3][1])
                wait("scalar", "qo", n["qo"])
                ea0 = op("scalar", lambda e: e.activation(
                    out=rn_tm[:], in_=z_ln2[:], func=AF.Identity,
                    accum_out=cols[:, 3:4]), "ac")
                wait("scalar", "ac", ea0)
                ea = op("scalar", lambda e: e.activation(
                    out=rn_tm[:], in_=z_ln2[:], func=AF.Square,
                    accum_out=cols[:, 5:6]), "ac")
                wait("vector", "ac", ea)
                edv = op("vector", lambda e: e.tensor_scalar(
                    out=cols[:, 4:5], in0=cols[:, 3:4], scalar1=1.0 / D,
                    scalar2=None, op0=OP.mult), "dv")
                op("vector", lambda e: e.tensor_scalar(
                    out=cols[:, 6:7], in0=cols[:, 5:6], scalar1=1.0 / D,
                    scalar2=None, op0=OP.mult))
                wait("vector", "dv", edv)
                c7 = op("vector", lambda e: e.tensor_scalar(
                    out=cols[:, 7:8], in0=cols[:, 4:5], scalar1=cols[:, 4:5],
                    scalar2=None, op0=OP.mult), "dv")
                wait("vector", "dv", c7)
                ev2 = op("vector", lambda e: e.tensor_tensor(
                    out=cols[:, 8:9], in0=cols[:, 6:7], in1=cols[:, 7:8],
                    op=OP.subtract), "dv")
                wait("scalar", "dv", ev2)
                a9 = op("scalar", lambda e: e.activation(
                    out=cols[:, 9:10], in_=cols[:, 8:9], func=AF.Ln,
                    bias=cst[:, 1:2]), "ac")
                wait("scalar", "ac", a9)
                er = op("scalar", lambda e: e.activation(
                    out=cols[:, 10:11], in_=cols[:, 9:10], func=AF.Exp,
                    scale=cst[:, 2:3]), "ac")
                wait("vector", "ac", er)
                c11 = op("vector", lambda e: e.tensor_scalar(
                    out=cols[:, 11:12], in0=cols[:, 4:5], scalar1=cols[:, 10:11],
                    scalar2=-1.0, op0=OP.mult, op1=OP.mult), "dv")
                wait("vector", "dv", c11)
                e5 = op("vector", lambda e: e.tensor_scalar(
                    out=rn_tm[:], in0=z_ln2[:], scalar1=cols[:, 10:11],
                    scalar2=cols[:, 11:12], op0=OP.mult, op1=OP.add), "dv")
                wait("gpsimd", "dv", e5)
                dma("gpsimd", "qo", out[sq, tok0 + tsl * P: tok0 + (tsl + 1) * P, :],
                    rn_tm[:])

            for th in range(2):
                for kh in range(KH + 1):
                    def stage(th=th, kh=kh):
                        if kh < KH:
                            gu_issue(th, kh)
                        if kh >= 1:
                            gu_drain(th, kh - 1)
                    units.append(stage)
                for kh in range(KH):
                    units.append(lambda th=th, kh=kh: down_unit(th, kh))
                units.append(lambda th=th: ln2_unit(th, 0))
                units.append(lambda th=th: ln2_unit(th, 1))
            return units

        h3 = h_all.rearrange("p (n t) -> p n t", n=NS)
        b3 = Brep.rearrange("p (n t) -> p n t", n=NS)

        def emit_scan(mt, units, reset_ev):
            cur_scope[0] = f"m{mt}_ph6"
            wait("vector", "ev", n["ev"])
            wait("vector", "dv", ev_sgh)
            wait("scalar", "dv", ev_sgh)
            if reset_ev:
                wait("vector", "sc", reset_ev)
                wait("scalar", "sc", reset_ev)
            ui = 0
            nu = len(units)
            for c in range(NE):
                vdx = op("vector", lambda e, c=c: e.tensor_tensor(
                    out=tap[:, 0:T], in0=dy_fm[:, c * T:(c + 1) * T],
                    in1=x_fm[:, c * T:(c + 1) * T], op=OP.mult), "dv")
                wait("vector", "dv", vdx)
                dxb = tap[:, 0:T].unsqueeze(1).broadcast_to([P, NS, T])
                i_ev = op("vector", lambda e, dxb=dxb: e.tensor_tensor(
                    out=h3, in0=dxb, in1=b3, op=OP.mult), "dv")
                for nn_ in range(NS):
                    kkg = len(scan_ev)
                    par = kkg % 2
                    if kkg >= 2:
                        wait("scalar", "dv", scan_ev[kkg - 2])
                    ad_v = op("scalar", lambda e, c=c, nn_=nn_, par=par: e.activation(
                        out=d_buf[:, par * T:(par + 1) * T],
                        in_=dy_fm[:, c * T:(c + 1) * T],
                        func=AF.Exp, scale=A_sb[:, c * NS + nn_:c * NS + nn_ + 1]), "ad")
                    wait("vector", "dv", i_ev if nn_ == 0 else scan_ev[-1])
                    wait("vector", "ad", ad_v)
                    sev = op("vector", lambda e, nn_=nn_, c=c, par=par:
                             e.tensor_tensor_scan(
                        out=h_all[:, nn_ * T:(nn_ + 1) * T],
                        data0=d_buf[:, par * T:(par + 1) * T],
                        data1=h_all[:, nn_ * T:(nn_ + 1) * T],
                        initial=carry[:, c * NS + nn_:c * NS + nn_ + 1],
                        op0=OP.mult, op1=OP.add), "dv")
                    scan_ev.append(sev)
                wait("vector", "dv", scan_ev[-1])
                vcc = op("vector", lambda e, c=c: e.tensor_copy(
                    out=carry[:, c * NS:(c + 1) * NS],
                    in_=h3[:, :, T - 1]), "dv")
                wait("vector", "dv", vcc)
                vt0 = op("vector", lambda e: e.tensor_tensor(
                    out=h_all[:], in0=h_all[:], in1=Crep[:], op=OP.mult), "dv")
                wait("vector", "dv", vt0)
                vt1 = op("vector", lambda e: e.tensor_tensor(
                    out=h_all[:, 0:8 * T], in0=h_all[:, 0:8 * T],
                    in1=h_all[:, 8 * T:16 * T], op=OP.add), "dv")
                wait("vector", "dv", vt1)
                vt2 = op("vector", lambda e: e.tensor_tensor(
                    out=h_all[:, 0:4 * T], in0=h_all[:, 0:4 * T],
                    in1=h_all[:, 4 * T:8 * T], op=OP.add), "dv")
                wait("vector", "dv", vt2)
                vt3 = op("vector", lambda e: e.tensor_tensor(
                    out=h_all[:, 0:2 * T], in0=h_all[:, 0:2 * T],
                    in1=h_all[:, 2 * T:4 * T], op=OP.add), "dv")
                wait("vector", "dv", vt3)
                op("vector", lambda e, c=c: e.tensor_tensor(
                    out=dy_fm[:, c * T:(c + 1) * T], in0=h_all[:, 0:T],
                    in1=h_all[:, T:2 * T], op=OP.add), "dv")
                tgt = (c + 1) * nu // NE
                while ui < tgt:
                    units[ui]()
                    ui += 1
            return n["dv"]

        def emit_ph7(mt, SCAN_DV):
            cur_scope[0] = f"m{mt}_ph7"
            ym_ev = {}
            guard_p = {0: max(guard_w[0], guard_w[1]), 1: max(guard_d[0], guard_d[1])}
            cur_buf = wbuf
            cur_vw = ("qw0", 0)
            cur_pi = 0
            for c in range(NE):
                par = c % 2
                sskip = tap[:, (1 + par) * T:(2 + par) * T]
                ytmp = tap[:, 0:T] if par == 0 else tap[:, 3 * T:4 * T]
                if par == 0:
                    pi = (c // 2) % 2
                    buf = wbuf if pi == 0 else dbuf
                    qsem = "qw0" if pi == 0 else "qd0"
                    wait("sync", "pe", guard_p[pi])
                    vw = dma("sync", qsem, buf[:],
                             sc_skip[c:c + 2].rearrange("c p d -> p c d"))
                    cur_vw = (qsem, vw)
                    cur_buf = buf
                    cur_pi = pi
                    wait("tensor", qsem, vw)
                psb = par
                pev = mm_group(psb, [
                    (lambda kd=kd, par=par, cur_buf=cur_buf:
                     cur_buf[:, par, kd * P:(kd + 1) * P],
                     lambda kd=kd: rn_fm[:, kd * T:(kd + 1) * T],
                     lambda psb=psb: ps[psb][:]) for kd in range(KD)],
                    extra_wait=("dv", SCAN_DV) if c < 2 else None)
                guard_p[cur_pi] = pev
                wait("scalar", "pe", pev)
                if c >= 2:
                    wait("scalar", "dv", ym_ev[c - 2])
                esk = op("scalar", lambda e, psb=psb, sskip=sskip: e.activation(
                    out=sskip, in_=ps[psb][:], func=AF.Silu), "ev")
                bank_free[psb] = _evs(esk)
                wait("vector", "ev", esk)
                y1 = op("vector", lambda e, c=c, ytmp=ytmp: e.tensor_scalar(
                    out=ytmp, in0=x_fm[:, c * T:(c + 1) * T],
                    scalar1=WD_sb[:, c:c + 1], scalar2=None, op0=OP.mult), "dv")
                wait("vector", "dv", y1)
                y2 = op("vector", lambda e, c=c, ytmp=ytmp: e.tensor_tensor(
                    out=ytmp, in0=ytmp, in1=dy_fm[:, c * T:(c + 1) * T], op=OP.add), "dv")
                wait("vector", "dv", y2)
                ym_ev[c] = op("vector", lambda e, c=c, ytmp=ytmp, sskip=sskip:
                              e.tensor_tensor(
                    out=dy_fm[:, c * T:(c + 1) * T], in0=ytmp, in1=sskip,
                    op=OP.mult), "dv")
            guard_w[0] = guard_w[1] = guard_p[0]
            guard_d[0] = guard_d[1] = guard_p[1]
            return n["dv"]

        def emit_ph8(mt, Y_DV):
            cur_scope[0] = f"m{mt}_ph8"
            sq, tok0 = seq_of(mt)
            if mt == 0:
                wait("sync", "qp", marks["out_qp"])
                wait("sync", "qb", marks["out_qb"])
                wait("scalar", "ac", max(ulast_cast[0], ulast_cast[1]))
            guard_p = {0: max(guard_w[0], guard_w[1]), 1: max(guard_d[0], guard_d[1])}
            for half in range(2):
                for cp in range(0, NE, 2):
                    pi = (half * (NE // 2) + cp // 2) % 2
                    buf = wbuf if pi == 0 else dbuf
                    qsem = "qw0" if pi == 0 else "qd0"
                    wait("sync", "pe", guard_p[pi])
                    vw = dma("sync", qsem, buf[:],
                             sc_out[cp:cp + 2].rearrange("c p d -> p c d"))
                    wait("tensor", qsem, vw)
                    for k in range(2):
                        c = cp + k
                        if half == 0 and c < 2:
                            wait("tensor", "dv", Y_DV)
                        for j in range(8):
                            tsl = 2 * half + j // 4
                            dg = j % 4
                            psb = j
                            stt, sp = (c == 0), (c == NE - 1)
                            if stt:
                                wait("tensor", bank_free[psb][0], bank_free[psb][1])
                            f = (lambda e, c=c, k=k, buf=buf, tsl=tsl, dg=dg, psb=psb,
                                 stt=stt, sp=sp:
                                 e.matmul(ps[psb][:],
                                          dy_fm[:, c * T + tsl * P: c * T + (tsl + 1) * P],
                                          buf[:, k, dg * 512:(dg + 1) * 512],
                                          start=stt, stop=sp))
                            if j == 7:
                                op("tensor", f, "pe")
                            else:
                                op("tensor", f)
                    guard_p[pi] = n["pe"]
                OUT_PE = n["pe"]
                for jt in range(2):
                    tsl = 2 * half + jt
                    wait("gpsimd", "dv", n["dv"])
                    va = dma("gpsimd", "qz", z_x[:],
                             resid[sq, tok0 + tsl * P: tok0 + (tsl + 1) * P, :])
                    wait("vector", "qz", va)
                    wait("vector", "pe", OUT_PE)
                    z0 = op("vector", lambda e: e.tensor_scalar(
                        out=z_x[:], in0=z_x[:], scalar1=2.0, scalar2=None,
                        op0=OP.mult), "dv")
                    wait("vector", "dv", z0)
                    for dg in range(4):
                        psb = jt * 4 + dg
                        bank_free[psb] = _vvs(op("vector", lambda e, psb=psb, dg=dg:
                                                 e.tensor_tensor(
                            out=z_x[:, dg * 512:(dg + 1) * 512],
                            in0=z_x[:, dg * 512:(dg + 1) * 512],
                            in1=ps[psb][:], op=OP.add), "vv"))
                    wait("scalar", bank_free[jt * 4 + 3][0], bank_free[jt * 4 + 3][1])
                    wait("scalar", "qx", n["qx"])
                    wait("scalar", "qo", n["qo"])
                    wait("scalar", "pe", st_["lastdma_rn_pe"])
                    ea0 = op("scalar", lambda e: e.activation(
                        out=rn_tm[:], in_=z_x[:], func=AF.Identity,
                        accum_out=cols[:, 3:4]), "ac")
                    wait("scalar", "ac", ea0)
                    ea = op("scalar", lambda e: e.activation(
                        out=rn_tm[:], in_=z_x[:], func=AF.Square,
                        accum_out=cols[:, 5:6]), "ac")
                    wait("vector", "ac", ea)
                    edv = op("vector", lambda e: e.tensor_scalar(
                        out=cols[:, 4:5], in0=cols[:, 3:4], scalar1=1.0 / D,
                        scalar2=None, op0=OP.mult), "dv")
                    op("vector", lambda e: e.tensor_scalar(
                        out=cols[:, 6:7], in0=cols[:, 5:6], scalar1=1.0 / D,
                        scalar2=None, op0=OP.mult))
                    wait("vector", "dv", edv)
                    c7 = op("vector", lambda e: e.tensor_scalar(
                        out=cols[:, 7:8], in0=cols[:, 4:5], scalar1=cols[:, 4:5],
                        scalar2=None, op0=OP.mult), "dv")
                    wait("vector", "dv", c7)
                    ev2 = op("vector", lambda e: e.tensor_tensor(
                        out=cols[:, 8:9], in0=cols[:, 6:7], in1=cols[:, 7:8],
                        op=OP.subtract), "dv")
                    wait("scalar", "dv", ev2)
                    a9 = op("scalar", lambda e: e.activation(
                        out=cols[:, 9:10], in_=cols[:, 8:9], func=AF.Ln,
                        bias=cst[:, 1:2]), "ac")
                    wait("scalar", "ac", a9)
                    er = op("scalar", lambda e: e.activation(
                        out=cols[:, 10:11], in_=cols[:, 9:10], func=AF.Exp,
                        scale=cst[:, 2:3]), "ac")
                    wait("vector", "ac", er)
                    c11 = op("vector", lambda e: e.tensor_scalar(
                        out=cols[:, 11:12], in0=cols[:, 4:5], scalar1=cols[:, 10:11],
                        scalar2=-1.0, op0=OP.mult, op1=OP.mult), "dv")
                    wait("vector", "dv", c11)
                    e4 = op("vector", lambda e: e.tensor_scalar(
                        out=rn_tm[:], in0=z_x[:], scalar1=cols[:, 10:11],
                        scalar2=cols[:, 11:12], op0=OP.mult, op1=OP.add), "dv")
                    wait("gpsimd", "dv", e4)
                    wait("gpsimd", "qx", n["qx"])
                    dma("gpsimd", "qx", sc_x1[tsl], rn_tm[:])
                    wait("tensor", "dv", e4)
                    if half == 0 and jt == 0:
                        wait("tensor", "pe", st_["x1_use_pe"])
                    for kd in range(KD):
                        psb = kd % 2
                        wait("tensor", bank_free[psb][0], bank_free[psb][1])
                        pv = op("tensor", lambda e, kd=kd, psb=psb: e.transpose(
                            ps[psb][:, 0:P], rn_tm[:, kd * P:(kd + 1) * P], ident[:]), "pe")
                        wait("scalar", "pe", pv)
                        bank_free[psb] = _evs(op("scalar", lambda e, kd=kd, tsl=tsl,
                                                 psb=psb: e.activation(
                            out=x1_fm[:, kd * T + tsl * P: kd * T + (tsl + 1) * P],
                            in_=ps[psb][:, 0:P], func=AF.Identity), "ev"))
                    st_["lastdma_rn_pe"] = n["pe"]
            guard_w[0] = guard_w[1] = guard_p[0]
            guard_d[0] = guard_d[1] = guard_p[1]

        rn_ev0 = emit_ph1(0)
        emit_ph2(0, ev_zero, rn_ev0)
        emit_ph5(0)
        emit_ph34(0)
        prev_units = build_prep_units()
        pending_reset = ev_zero
        for mt in range(MT):
            if mt == 1:
                wait("sync", "qp", marks["ffn_qp"])
                wait("sync", "qb", marks["ffn_qb"])
                wait("sync", "qw0", n["qw0"])
                wait("sync", "qw1", n["qw1"])
            SCAN_DV = emit_scan(mt, prev_units, pending_reset)
            Y_DV = emit_ph7(mt, SCAN_DV)
            emit_ph8(mt, Y_DV)
            prev_units = build_ffn_units(mt)
            if mt + 1 < MT:
                nxt = mt + 1
                if nxt % (L // T) == 0:
                    wait("gpsimd", "dv", n["dv"])
                    op("gpsimd", lambda e: e.memset(carry[:], 0.0))
                    reset_ev = op("gpsimd", lambda e: e.memset(tails[:], 0.0), "sc")
                else:
                    reset_ev = None
                pending_reset = reset_ev
                rn_evn = emit_ph1(nxt)
                emit_ph2(nxt, reset_ev, rn_evn)
                emit_ph5(nxt)
                emit_ph34(nxt)
        cur_scope[0] = "tail_ffn"
        for u in prev_units:
            u()

        wait("gpsimd", "qo", n["qo"])
        wait("sync", "qo", n["qo"])

        from contextlib import nullcontext

        def register(engname):
            def body(eng):
                for scope, f in plan[engname]:
                    cm = nc.named_scope(scope) if scope else nullcontext()
                    with cm:
                        f(eng)
            return body

        block.sync(register("sync"))
        block.gpsimd(register("gpsimd"))
        block.tensor(register("tensor"))
        block.vector(register("vector"))
        block.scalar(register("scalar"))

    return nc


def kernel(**inputs):
    h = _prep_inputs(inputs)
    resid = np.ascontiguousarray(np.asarray(inputs["L0_attn_resid"], dtype=np.float32))
    nc = build()
    in_maps = []
    for core in range(NCORES):
        in_maps.append({
            "resid": resid[core * SEQS:(core + 1) * SEQS],
            "inwT": h["inwT"], "skipwT": h["skipwT"], "outwT": h["outwT"],
            "wd1T": h["wd1T"], "wd2T": h["wd2T"], "wBT": h["wBT"], "wCT": h["wCT"],
            "gwT": h["gwT"], "uwT": h["uwT"], "dwT": h["dwT"],
            "sgate_c": h["sgate_c"], "sup_c": h["sup_c"], "s_down": h["s_down"],
            "normw": h["normw"], "convw": h["convw"], "convb": h["convb"],
            "wd2b": h["wd2b"], "WD": h["WD"], "Alog": h["Alog"],
        })
    res = run_bass_kernel_spmd(nc, in_maps, core_ids=list(range(NCORES)))
    outs = [np.asarray(r["out"]) for r in res.results]
    return np.concatenate(outs, axis=0).reshape(16, L, D).astype(np.float32)


# version 17
# speedup vs baseline: 1.1758x; 1.1758x over previous
import numpy as np
import concourse.bass as bass
import concourse.mybir as mybir
from concourse.bass_utils import run_bass_kernel_spmd

F32 = mybir.dt.float32
BF16 = mybir.dt.bfloat16
I32 = mybir.dt.int32
AF = mybir.ActivationFunctionType
OP = mybir.AluOpType
AX = mybir.AxisListType

P = 128
D = 2048
E = 4096
HID = 5632
R = 128
NS = 16
KCONV = 4
SEQS = 2
L = 2048
T = 512
MT = SEQS * (L // T)
TSB = T // P
KD = D // P
NE = E // P
KH = HID // P
EPS = 1e-5
NCORES = 8
HH = HID // 2

DMA_SEMS = ["qw0", "qw1", "qd0", "qd1", "qz", "qo", "qx", "qb", "qp"]
ENG_SEMS = ["pe", "ev", "vv", "dv", "ac", "sc", "ad"]


def _rearr_col(v, nchunk):
    return np.ascontiguousarray(np.asarray(v).reshape(nchunk, P).T).astype(np.float32)


def _prep_inputs(inp):
    asc = np.ascontiguousarray
    h = {}
    h["inwT"] = asc(np.asarray(inp["L0_attn_in_w"].T, np.float32))
    h["skipwT"] = asc(np.asarray(inp["L0_attn_skip_w"].T, np.float32))
    h["outwT"] = asc(np.asarray(inp["L0_attn_out_w"].T, np.float32))
    h["wd1T"] = asc(np.asarray(inp["L0_attn_wd1"].T, np.float32))
    h["wd2T"] = asc(np.asarray(inp["L0_attn_wd2"].T, np.float32))
    h["wBT"] = asc(np.asarray(inp["L0_attn_wB"].T, np.float32))
    h["wCT"] = asc(np.asarray(inp["L0_attn_wC"].T, np.float32))
    h["gwT"] = asc(np.asarray(inp["L0_ffn_wq_gate"].T, np.int32))
    h["uwT"] = asc(np.asarray(inp["L0_ffn_wq_up"].T, np.int32))
    h["dwT"] = asc(np.asarray(inp["L0_ffn_wq_down"].T, np.int32))
    h["sgate_c"] = _rearr_col(inp["L0_ffn_s_gate"], KH)
    h["sup_c"] = _rearr_col(inp["L0_ffn_s_up"], KH)
    h["s_down"] = asc(np.asarray(inp["L0_ffn_s_down"], np.float32).reshape(1, D))
    h["normw"] = _rearr_col(inp["L0_attn_norm_w"], KD)
    h["convw"] = asc(np.asarray(inp["L0_attn_conv_w"], np.float32).reshape(NE, P, KCONV)
                     .transpose(1, 0, 2).reshape(P, NE * KCONV))
    h["convb"] = _rearr_col(inp["L0_attn_conv_b"], NE)
    h["wd2b"] = _rearr_col(inp["L0_attn_wd2_b"], NE)
    h["WD"] = _rearr_col(inp["L0_attn_W_D"], NE)
    h["Alog"] = asc(np.asarray(inp["L0_attn_A_log"], np.float32).reshape(NE, P, NS)
                    .transpose(1, 0, 2).reshape(P, NE * NS))
    return h


def build():
    nc = bass.Bass(target_bir_lowering=False)

    resid = nc.dram_tensor("resid", [SEQS, L, D], F32, kind="ExternalInput")
    inwT = nc.dram_tensor("inwT", [D, E], F32, kind="ExternalInput")
    skipwT = nc.dram_tensor("skipwT", [D, E], F32, kind="ExternalInput")
    outwT = nc.dram_tensor("outwT", [E, D], F32, kind="ExternalInput")
    wd1T = nc.dram_tensor("wd1T", [E, R], F32, kind="ExternalInput")
    wd2T = nc.dram_tensor("wd2T", [R, E], F32, kind="ExternalInput")
    wBT = nc.dram_tensor("wBT", [E, NS], F32, kind="ExternalInput")
    wCT = nc.dram_tensor("wCT", [E, NS], F32, kind="ExternalInput")
    gwT = nc.dram_tensor("gwT", [D, HID], I32, kind="ExternalInput")
    uwT = nc.dram_tensor("uwT", [D, HID], I32, kind="ExternalInput")
    dwT = nc.dram_tensor("dwT", [HID, D], I32, kind="ExternalInput")
    sgate_in = nc.dram_tensor("sgate_c", [P, KH], F32, kind="ExternalInput")
    sup_in = nc.dram_tensor("sup_c", [P, KH], F32, kind="ExternalInput")
    s_down = nc.dram_tensor("s_down", [1, D], F32, kind="ExternalInput")
    normw_in = nc.dram_tensor("normw", [P, KD], F32, kind="ExternalInput")
    convw_in = nc.dram_tensor("convw", [P, NE * KCONV], F32, kind="ExternalInput")
    convb_in = nc.dram_tensor("convb", [P, NE], F32, kind="ExternalInput")
    wd2b_in = nc.dram_tensor("wd2b", [P, NE], F32, kind="ExternalInput")
    WD_in = nc.dram_tensor("WD", [P, NE], F32, kind="ExternalInput")
    Alog_in = nc.dram_tensor("Alog", [P, NE * NS], F32, kind="ExternalInput")
    out = nc.dram_tensor("out", [SEQS, L, D], F32, kind="ExternalOutput")

    sc_in = nc.dram_tensor("sc_in", [NE, P, KD * P], BF16)
    sc_skip = nc.dram_tensor("sc_skip", [NE, P, KD * P], BF16)
    sc_out = nc.dram_tensor("sc_out", [NE, P, D], BF16)
    sc_g = nc.dram_tensor("sc_g", [KH, P, KD * P], BF16)
    sc_u = nc.dram_tensor("sc_u", [KH, P, KD * P], BF16)
    sc_d = nc.dram_tensor("sc_d", [KH, P, D], BF16)
    sc_bc = nc.dram_tensor("sc_bc", [2 * NS, T], BF16)
    sc_wd = nc.dram_tensor("sc_wd", [2, P, E], BF16)
    sc_x1 = nc.dram_tensor("sc_x1", [TSB, P, D], F32)

    from contextlib import ExitStack
    with ExitStack() as ctx:
        En = ctx.enter_context

        def sbt(name, shape, dt=BF16):
            return En(nc.sbuf_tensor(name, shape, dt))

        consts = sbt("consts", [P, 3232])
        ident = consts[:, 0:256].bitcast(F32)
        A_sb = consts[:, 256:1280].bitcast(F32)
        convw_sb = consts[:, 1280:1536].bitcast(F32)
        convb_sb = consts[:, 1536:1600].bitcast(F32)
        wd2b_sb = consts[:, 1600:1664].bitcast(F32)
        WD_sb = consts[:, 1664:1728].bitcast(F32)
        normw_sb = consts[:, 1728:1760].bitcast(F32)
        cols = consts[:, 1760:1792].bitcast(F32)
        cst = consts[:, 1792:1808].bitcast(F32)
        wB_sb = consts[:, 1808:2320]
        wC_sb = consts[:, 2320:2832]
        ones1 = consts[0:1, 2832:2960]
        tails = consts[:, 2960:3056]
        sgh_sb = consts[:, 3056:3144].bitcast(F32)
        su_sb = consts[:, 3144:3232].bitcast(F32)

        x_fm = sbt("x_fm", [P, NE * T])
        dy_fm = sbt("dy_fm", [P, NE * T])
        z_x = x_fm[:, 0:4096].bitcast(F32)
        rn_fm = sbt("rn_fm", [P, KD * T])
        rn_tm = sbt("rn_tm", [P, D], F32)
        wbuf = sbt("wbuf", [P, 2, KD * P])
        dbuf = sbt("dbuf", [P, 2, D])

        arena = sbt("arena", [P, 49728])
        Brep = arena[:, 0:NS * T]
        Crep = arena[:, NS * T:2 * NS * T]
        h_all = arena[:, 16384:16384 + NS * T]
        d_buf = arena[:, 25088:25088 + 2 * T]
        carry = arena[:, 26112:26112 + 2 * NE * NS].bitcast(F32)
        tap = arena[:, 27136:27136 + 4 * T]
        brow = arena[0:1, 27136:27136 + 4 * T]
        bc_nt = arena[0:2 * NS, 29184:29184 + T]
        bc_t_f = arena[:, 29696:29760].bitcast(F32)
        gt_b = arena[:, 29760:29760 + T // 2]
        up_b = arena[:, 30016:30016 + T // 2]
        x1_fm = arena[:, 30272:30272 + KD * T]
        gu_half = arena[:, 38464:38464 + KH * (T // 2)]
        z_ln2 = arena[:, 38464 + 7168:38464 + 11264].bitcast(F32)
        rms_f = arena[:, 38464:38464 + 2 * D].bitcast(F32)
        conv_f = arena[:, 42560:43584].bitcast(F32)
        xin_buf = arena[:, 43584:43584 + KCONV - 1 + T]
        t1_sb = arena[:, 44168:44168 + T]
        pbufA = arena[:, 0:8192].bitcast(F32)
        pbufB = arena[:, 8192:16384].bitcast(F32)
        pout16A = arena[:, 16384:21760]
        pout16B = arena[:, 21760:27136]
        ubufA = arena[:, 30272:35904]
        ubufB = arena[:, 35904:41536]
        uoutA = arena[:, 41536:44352]
        uoutB = arena[:, 44352:47168]
        srepD_sb = arena[:, 47168:49216]

        ps = [En(nc.psum_tensor(f"ps{i}", [P, 512], F32)) for i in range(8)]

        sems = {}
        for name in DMA_SEMS + ENG_SEMS:
            sems[name] = En(nc.semaphore(name))

        block = En(nc.Block())
        plan = {k: [] for k in ["sync", "gpsimd", "tensor", "vector", "scalar"]}
        n = {k: 0 for k in sems}
        cur_scope = [None]

        def emit(eng, fn):
            plan[eng].append((cur_scope[0], fn))

        def wait(eng, sem, val):
            if val is None or val <= 0:
                return
            emit(eng, lambda e, sem=sem, val=val: e.wait_ge(sems[sem], val))

        def dma(eng, sem, dst, src):
            n[sem] += 16
            v = n[sem]
            emit(eng, lambda e, dst=dst, src=src, sem=sem:
                 e.dma_start(out=dst, in_=src).then_inc(sems[sem], 16))
            return v

        def op(eng, fn, sem=None):
            if sem is None:
                emit(eng, fn)
                return None
            n[sem] += 1
            v = n[sem]
            emit(eng, lambda e, fn=fn, sem=sem: fn(e).then_inc(sems[sem], 1))
            return v

        bank_free = {i: ("ev", 0) for i in range(8)}

        def _evs(v):
            return ("ev", v)

        def _vvs(v):
            return ("vv", v)

        def mm_group(psb, tiles, extra_wait=None):
            first = True
            nt = len(tiles)
            res = None
            for j, (lf, rf, of) in enumerate(tiles):
                if first:
                    wait("tensor", bank_free[psb][0], bank_free[psb][1])
                    if extra_wait is not None:
                        wait("tensor", extra_wait[0], extra_wait[1])
                    first = False
                st, sp = (j == 0), (j == nt - 1)
                f = (lambda e, lf=lf, rf=rf, of=of, st=st, sp=sp:
                     e.matmul(of(), lf(), rf(), start=st, stop=sp))
                if sp:
                    res = op("tensor", f, "pe")
                else:
                    op("tensor", f)
            return res

        ev_ms = op("gpsimd", lambda e: e.memset(ident[:], 0.0), "sc")
        wait("gpsimd", "sc", ev_ms)
        ev_ident = op("gpsimd", lambda e: e.affine_select(
            out=ident[:], in_=ident[:], compare_op=OP.not_equal, fill=1.0,
            base=0, pattern=[[-1, P]], channel_multiplier=1), "sc")
        ev_ones = op("gpsimd", lambda e: e.memset(ones1[:], 1.0), "sc")
        op("gpsimd", lambda e: e.memset(cst[:, 0:1], -128.0))
        op("gpsimd", lambda e: e.memset(cst[:, 1:2], EPS))
        op("gpsimd", lambda e: e.memset(cst[:, 2:3], -0.5))
        op("gpsimd", lambda e: e.memset(cst[:, 3:4], 1.0 / D))
        ev_cst = op("gpsimd", lambda e: e.memset(cst[:, 4:5], 0.5), "sc")

        dma("sync", "qw0", normw_sb[:], normw_in[:])
        dma("sync", "qw0", convw_sb[:], convw_in[:])
        dma("sync", "qw0", convb_sb[:], convb_in[:])
        dma("sync", "qw0", wd2b_sb[:], wd2b_in[:])
        dma("sync", "qw0", WD_sb[:], WD_in[:])
        dma("sync", "qw0", sgh_sb[:], sgate_in[:])
        dma("sync", "qw0", su_sb[:], sup_in[:])
        v = dma("sync", "qw0", A_sb[:], Alog_in[:])
        wait("sync", "qw0", v)
        wait("scalar", "qw0", v)
        ev_aexp = op("scalar", lambda e: e.activation(
            out=A_sb[:], in_=A_sb[:], func=AF.Exp), "ac")
        wait("vector", "ac", ev_aexp)
        ev_A = op("vector", lambda e: e.tensor_scalar(
            out=A_sb[:], in0=A_sb[:], scalar1=-1.0, scalar2=None, op0=OP.mult), "dv")
        ev_sgh = op("vector", lambda e: e.tensor_scalar(
            out=sgh_sb[:], in0=sgh_sb[:], scalar1=0.5, scalar2=None, op0=OP.mult), "dv")

        for c in range(NE):
            dma("sync", "qw0", pbufA[:, c * R:(c + 1) * R], wd1T[c * P:(c + 1) * P, :])
        wait("sync", "qw0", n["qw0"])
        wait("scalar", "qw0", n["qw0"])
        e1 = op("scalar", lambda e: e.activation(
            out=pout16A[:, 0:NE * R], in_=pbufA[:, 0:NE * R], func=AF.Identity), "ac")
        wait("gpsimd", "ac", e1)
        dma("gpsimd", "qp", sc_wd[0, :, 0:NE * R], pout16A[:, 0:NE * R])
        v = dma("sync", "qw0", pbufB[:, 0:E], wd2T[:])
        wait("sync", "qw0", v)
        wait("scalar", "qw0", v)
        e2 = op("scalar", lambda e: e.activation(
            out=pout16B[:, 0:E], in_=pbufB[:, 0:E], func=AF.Identity), "ac")
        wait("gpsimd", "ac", e2)
        dma("gpsimd", "qb", sc_wd[1], pout16B[:, 0:E])
        wait("sync", "ac", e1)
        for c in range(NE):
            dma("sync", "qw0", pbufA[:, c * NS:(c + 1) * NS], wBT[c * P:(c + 1) * P, :])
        wait("vector", "qw0", n["qw0"])
        ebc1 = op("vector", lambda e: e.tensor_copy(
            out=wB_sb[:], in_=pbufA[:, 0:NE * NS]), "dv")
        wait("sync", "dv", ebc1)
        for c in range(NE):
            dma("sync", "qw0", pbufA[:, c * NS:(c + 1) * NS], wCT[c * P:(c + 1) * P, :])
        wait("vector", "qw0", n["qw0"])
        ev_smallw = op("vector", lambda e: e.tensor_copy(
            out=wC_sb[:], in_=pbufA[:, 0:NE * NS]), "dv")
        wait("sync", "dv", ev_smallw)

        last_cast = {0: e1, 1: e2}
        last_store = {0: n["qp"], 1: n["qb"]}

        def cast_tile_w1(src_rows, scratch, kd, par):
            qs = "qw0" if par == 0 else "qw1"
            ss = "qp" if par == 0 else "qb"
            pb = pbufA if par == 0 else pbufB
            po = pout16A if par == 0 else pout16B
            wait("sync", "ac", last_cast[par])
            v = dma("sync", qs, pb[:, 0:E], src_rows)
            wait("scalar", qs, v)
            wait("scalar", ss, last_store[par])
            ec = op("scalar", lambda e, pb=pb, po=po: e.activation(
                out=po[:, 0:E], in_=pb[:, 0:E], func=AF.Identity), "ac")
            last_cast[par] = ec
            wait("gpsimd", "ac", ec)
            for g in range(4):
                dma("gpsimd", ss,
                    scratch[g * 8:(g + 1) * 8, :, kd * P:(kd + 1) * P]
                    .rearrange("c p j -> p c j"),
                    po[:, g * 8 * P:(g + 1) * 8 * P].rearrange("p (c j) -> p c j", j=P))
            last_store[par] = n[ss]

        ti = 0
        for scratch, src in [(sc_in, inwT), (sc_skip, skipwT)]:
            for kd in range(KD):
                cast_tile_w1(src[kd * P:(kd + 1) * P, :], scratch, kd, ti % 2)
                ti += 1
        IN_SKIP_QP = n["qp"]
        IN_SKIP_QB = n["qb"]

        wait("gpsimd", "ac", max(last_cast[0], last_cast[1]))
        op("gpsimd", lambda e: e.memset(carry[:], 0.0))
        ev_zero = op("gpsimd", lambda e: e.memset(tails[:], 0.0), "sc")

        marks = {}
        ulast_cast = {0: 0, 1: 0}
        ulast_store = {0: 0, 1: 0}
        upar = [0]
        ust = {"srepd_pe": 0, "srepd_sc": 0}

        def u_srepd():
            prow_f = arena[0:1, 30272:34368].bitcast(F32)
            prow16 = uoutA[0:1, 0:D]
            wait("sync", "qw0", n["qw0"])
            v = dma("sync", "qw0", prow_f[:], s_down[:])
            wait("scalar", "qw0", v)
            ec = op("scalar", lambda e: e.activation(
                out=prow16[:], in_=prow_f[:], func=AF.Identity), "ac")
            ulast_cast[0] = ec
            wait("tensor", "ac", ec)
            wait("tensor", "sc", ev_ones)
            for b_ in range(4):
                wait("tensor", bank_free[7][0], bank_free[7][1])
                pv = op("tensor", lambda e, b_=b_: e.matmul(
                    ps[7][:], ones1[0:1, :], prow16[0:1, b_ * 512:(b_ + 1) * 512],
                    start=True, stop=True), "pe")
                ust["srepd_pe"] = pv
                wait("scalar", "pe", pv)
                bank_free[7] = _evs(op("scalar", lambda e, b_=b_: e.activation(
                    out=srepD_sb[:, b_ * 512:(b_ + 1) * 512], in_=ps[7][:],
                    func=AF.Identity), "ev"))
            ust["srepd_sc"] = n["ev"]

        def u_out(c):
            par = upar[0]
            upar[0] ^= 1
            qs = "qw0" if par == 0 else "qw1"
            ss = "qp" if par == 0 else "qb"
            ub = (ubufA if par == 0 else ubufB).bitcast(F32)
            uo = uoutA if par == 0 else uoutB
            wait("sync", "ac", ulast_cast[par])
            if c == 0:
                wait("sync", "pe", ust["srepd_pe"])
            v = dma("sync", qs, ub[:, 0:D], outwT[c * P:(c + 1) * P, :])
            wait("scalar", qs, v)
            wait("scalar", ss, ulast_store[par])
            if c == 0:
                wait("scalar", "ev", ust["srepd_sc"])
            ec = op("scalar", lambda e, ub=ub, uo=uo: e.activation(
                out=uo[:, 0:D], in_=ub[:, 0:D], func=AF.Identity), "ac")
            ulast_cast[par] = ec
            wait("gpsimd", "ac", ec)
            dma("gpsimd", ss, sc_out[c], uo[:, 0:D])
            ulast_store[par] = n[ss]
            if c == NE - 1:
                marks["out_qp"] = n["qp"]
                marks["out_qb"] = n["qb"]

        def u_q(src, scratch, kd, hh):
            par = upar[0]
            upar[0] ^= 1
            qs = "qw0" if par == 0 else "qw1"
            ss = "qp" if par == 0 else "qb"
            ub = (ubufA if par == 0 else ubufB).bitcast(I32)
            uo = uoutA if par == 0 else uoutB
            wait("sync", "ac", ulast_cast[par])
            v = dma("sync", qs, ub[:, 0:HH], src[kd * P:(kd + 1) * P, hh * HH:(hh + 1) * HH])
            wait("scalar", qs, v)
            wait("scalar", ss, ulast_store[par])
            ec = op("scalar", lambda e, ub=ub, uo=uo: e.activation(
                out=uo[:, 0:HH], in_=ub[:, 0:HH], func=AF.Identity,
                bias=cst[:, 0:1]), "ac")
            ulast_cast[par] = ec
            wait("gpsimd", "ac", ec)
            m0 = hh * (KH // 2)
            for g in range(2):
                dma("gpsimd", ss,
                    scratch[m0 + g * 11:m0 + (g + 1) * 11, :, kd * P:(kd + 1) * P]
                    .rearrange("m p j -> p m j"),
                    uo[:, g * 11 * P:(g + 1) * 11 * P].rearrange("p (m j) -> p m j", j=P))
            ulast_store[par] = n[ss]

        def u_down(kh):
            par = upar[0]
            upar[0] ^= 1
            qs = "qw0" if par == 0 else "qw1"
            ss = "qp" if par == 0 else "qb"
            ub = (ubufA if par == 0 else ubufB).bitcast(I32)
            uo = uoutA if par == 0 else uoutB
            wait("sync", "ac", ulast_cast[par])
            v = dma("sync", qs, ub[:, 0:D], dwT[kh * P:(kh + 1) * P, :])
            wait("scalar", qs, v)
            wait("scalar", ss, ulast_store[par])
            ec = op("scalar", lambda e, ub=ub, uo=uo: e.activation(
                out=uo[:, 0:D], in_=ub[:, 0:D], func=AF.Identity,
                bias=cst[:, 0:1]), "ac")
            ulast_cast[par] = ec
            wait("gpsimd", "ac", ec)
            if kh == 0:
                wait("gpsimd", "ev", ust["srepd_sc"])
            gv = op("gpsimd", lambda e, uo=uo: e.tensor_tensor(
                out=uo[:, 0:D], in0=uo[:, 0:D], in1=srepD_sb[:], op=OP.mult), "sc")
            ulast_cast[par] = max(ulast_cast[par], ec)
            wait("gpsimd", "sc", gv)
            dma("gpsimd", ss, sc_d[kh], uo[:, 0:D])
            ulast_store[par] = n[ss]
            if kh == KH - 1:
                marks["ffn_qp"] = n["qp"]
                marks["ffn_qb"] = n["qb"]

        def build_prep_units():
            units = [u_srepd]
            for c in range(NE):
                units.append(lambda c=c: u_out(c))
            for src, scratch in [(gwT, sc_g), (uwT, sc_u)]:
                for kd in range(KD):
                    for hh in range(2):
                        units.append(lambda src=src, scratch=scratch, kd=kd, hh=hh:
                                     u_q(src, scratch, kd, hh))
            for kh in range(KH):
                units.append(lambda kh=kh: u_down(kh))
            return units

        guard_w = {0: 0, 1: 0}
        guard_d = {0: 0, 1: 0}
        scan_ev = []
        st_ = {"lastdma_rn_pe": 0, "x1_use_pe": 0, "prev_gu": 0}
        TH = T // 2

        def seq_of(mt):
            return mt // (L // T), (mt % (L // T)) * T

        def emit_ph1(mt):
            cur_scope[0] = f"m{mt}_ph1"
            sq, tok0 = seq_of(mt)
            e3_prev = None
            for ts in range(TSB):
                wait("gpsimd", "dv", n["dv"])
                va = dma("gpsimd", "qz", z_x[:],
                         resid[sq, tok0 + ts * P: tok0 + (ts + 1) * P, :])
                wait("scalar", "qz", va)
                if e3_prev:
                    wait("scalar", "dv", e3_prev)
                if mt == 0 and ts == 0:
                    wait("scalar", "sc", ev_cst)
                a1 = op("scalar", lambda e: e.activation(
                    out=rms_f[:], in_=z_x[:], func=AF.Square, accum_out=cols[:, 0:1]), "ac")
                wait("scalar", "ac", a1)
                a2 = op("scalar", lambda e: e.activation(
                    out=cols[:, 1:2], in_=cols[:, 0:1], func=AF.Ln,
                    scale=cst[:, 3:4], bias=cst[:, 1:2]), "ac")
                wait("scalar", "ac", a2)
                e2 = op("scalar", lambda e: e.activation(
                    out=cols[:, 2:3], in_=cols[:, 1:2], func=AF.Exp, scale=cst[:, 2:3]), "ac")
                wait("vector", "ac", e2)
                e3 = op("vector", lambda e: e.tensor_scalar(
                    out=rms_f[:], in0=z_x[:], scalar1=cols[:, 2:3], scalar2=None,
                    op0=OP.mult), "dv")
                e3_prev = e3
                wait("tensor", "dv", e3)
                if mt == 0 and ts == 0:
                    wait("tensor", "sc", ev_ident)
                for kd in range(KD):
                    psb = 2 + kd % 2
                    wait("tensor", bank_free[psb][0], bank_free[psb][1])
                    pv = op("tensor", lambda e, kd=kd, psb=psb: e.transpose(
                        ps[psb][:, 0:P], rms_f[:, kd * P:(kd + 1) * P], ident[:]), "pe")
                    wait("scalar", "pe", pv)
                    bank_free[psb] = _evs(op("scalar", lambda e, kd=kd, ts=ts, psb=psb:
                                             e.activation(
                        out=rn_fm[:, kd * T + ts * P: kd * T + (ts + 1) * P],
                        in_=ps[psb][:, 0:P], func=AF.Identity,
                        scale=normw_sb[:, kd:kd + 1]), "ev"))
            return n["ev"]

        def emit_ph2(mt, reset_ev, rn_ev):
            cur_scope[0] = f"m{mt}_ph2"
            if reset_ev:
                wait("vector", "sc", reset_ev)
            if mt == 0:
                wait("sync", "qp", IN_SKIP_QP)
                wait("sync", "qb", IN_SKIP_QB)
            prev_sil = None
            guard_p = {0: max(guard_w[0], guard_w[1]), 1: max(guard_d[0], guard_d[1])}
            for c in range(NE):
                par = c % 2
                if par == 0:
                    pi = (c // 2) % 2
                    buf = wbuf if pi == 0 else dbuf
                    qsem = "qw0" if pi == 0 else "qd0"
                    wait("sync", "pe", guard_p[pi])
                    vw = dma("sync", qsem, buf[:],
                             sc_in[c:c + 2].rearrange("c p d -> p c d"))
                    cur_vw = (qsem, vw)
                    cur_buf = buf
                    cur_pi = pi
                if par == 0:
                    wait("tensor", cur_vw[0], cur_vw[1])
                psb = par
                pev = mm_group(psb, [
                    (lambda kd=kd, par=par, cur_buf=cur_buf:
                     cur_buf[:, par, kd * P:(kd + 1) * P],
                     lambda kd=kd: rn_fm[:, kd * T:(kd + 1) * T],
                     lambda psb=psb: ps[psb][:]) for kd in range(KD)],
                    extra_wait=("ev", rn_ev) if c < 2 else None)
                guard_p[cur_pi] = pev
                wait("scalar", "pe", pev)
                if prev_sil:
                    wait("scalar", "dv", prev_sil)
                eev = op("scalar", lambda e, psb=psb: e.activation(
                    out=xin_buf[:, KCONV - 1:KCONV - 1 + T], in_=ps[psb][:],
                    func=AF.Identity), "ev")
                bank_free[psb] = _evs(eev)
                wait("vector", "ev", eev)
                v1 = op("vector", lambda e, c=c: e.tensor_copy(
                    out=xin_buf[:, 0:KCONV - 1],
                    in_=tails[:, c * (KCONV - 1):(c + 1) * (KCONV - 1)]), "dv")
                wait("vector", "dv", v1)
                op("vector", lambda e, c=c: e.tensor_copy(
                    out=tails[:, c * (KCONV - 1):(c + 1) * (KCONV - 1)],
                    in_=xin_buf[:, T:T + KCONV - 1]))
                acc = tap[:, 3 * T:4 * T]
                pv_ = op("vector", lambda e, c=c, acc=acc: e.tensor_scalar(
                    out=acc, in0=xin_buf[:, 0:T],
                    scalar1=convw_sb[:, c * KCONV:c * KCONV + 1],
                    scalar2=None, op0=OP.mult), "dv")
                for k in range(1, KCONV):
                    wait("vector", "dv", pv_)
                    pv_ = op("vector", lambda e, c=c, k=k, acc=acc: e.scalar_tensor_tensor(
                        out=acc, in0=xin_buf[:, k:k + T],
                        scalar=convw_sb[:, c * KCONV + k:c * KCONV + k + 1],
                        in1=acc, op0=OP.mult, op1=OP.add), "dv")
                edv = pv_
                prev_sil = edv
                wait("scalar", "dv", edv)
                op("scalar", lambda e, c=c, acc=acc: e.activation(
                    out=x_fm[:, c * T:(c + 1) * T], in_=acc, func=AF.Silu,
                    bias=convb_sb[:, c:c + 1]), "ac")
            guard_w[0] = guard_w[1] = guard_p[0]
            guard_d[0] = guard_d[1] = guard_p[1]

        def emit_ph34(mt):
            cur_scope[0] = f"m{mt}_ph34"
            wdv = wbuf[:].rearrange("p a b -> p (a b)")
            wait("sync", "pe", guard_w[0])
            wait("sync", "pe", guard_w[1])
            vw1 = dma("sync", "qw0", wdv[:, 0:NE * P], sc_wd[0, :, 0:NE * P])
            wait("tensor", "qw0", vw1)
            X_AC = n["ac"]
            pev = mm_group(0, [
                (lambda c=c: wdv[:, c * P:(c + 1) * P],
                 lambda c=c: x_fm[:, c * T:(c + 1) * T],
                 lambda: ps[0][:]) for c in range(NE)],
                extra_wait=("ac", X_AC))
            guard_w[0] = pev
            guard_w[1] = pev
            wait("scalar", "pe", pev)
            e_t1 = op("scalar", lambda e: e.activation(
                out=t1_sb, in_=ps[0][:], func=AF.Identity), "ev")
            bank_free[0] = _evs(e_t1)
            wait("sync", "pe", pev)
            vw2 = dma("sync", "qw0", wdv[:, 0:E], sc_wd[1, :, 0:E])
            wait("tensor", "qw0", vw2)
            prev_ln = None
            for c in range(NE):
                psb = c % 2
                wait("tensor", bank_free[psb][0], bank_free[psb][1])
                if c == 0:
                    wait("tensor", "ev", e_t1)
                pev = op("tensor", lambda e, c=c, psb=psb: e.matmul(
                    ps[psb][:], wdv[:, c * P:(c + 1) * P], t1_sb,
                    start=True, stop=True), "pe")
                wait("scalar", "pe", pev)
                if prev_ln:
                    wait("scalar", "ev", prev_ln)
                a4 = op("scalar", lambda e, c=c, psb=psb: e.activation(
                    out=conv_f[:, 0:T], in_=ps[psb][:], func=AF.Exp,
                    bias=wd2b_sb[:, c:c + 1]), "ac")
                wait("scalar", "ac", a4)
                lv = op("scalar", lambda e, c=c: e.activation(
                    out=dy_fm[:, c * T:(c + 1) * T], in_=conv_f[:, 0:T],
                    func=AF.Ln, bias=1.0), "ev")
                bank_free[psb] = _evs(lv)
                prev_ln = lv
            guard_w[0] = n["pe"]
            guard_w[1] = n["pe"]

        def emit_ph5(mt):
            cur_scope[0] = f"m{mt}_ph5"
            X_AC = n["ac"]
            if mt == 0:
                wait("tensor", "dv", ev_smallw)
            for ts in range(TSB):
                for wi, (wsb, psb, off) in enumerate([(wB_sb, 2, 0), (wC_sb, 3, NS)]):
                    pev = mm_group(psb, [
                        (lambda c=c, ts=ts: x_fm[:, c * T + ts * P: c * T + (ts + 1) * P],
                         lambda c=c, wsb=wsb: wsb[:, c * NS:(c + 1) * NS],
                         lambda psb=psb: ps[psb][:, 0:NS]) for c in range(NE)],
                        extra_wait=("ac", X_AC) if ts == 0 and wi == 0 else None)
                    wait("scalar", "pe", pev)
                    bank_free[psb] = _evs(op("scalar", lambda e, psb=psb, off=off:
                                             e.activation(
                        out=bc_t_f[:, off:off + NS], in_=ps[psb][:, 0:NS],
                        func=AF.Identity), "ev"))
                wait("tensor", "ev", n["ev"])
                wait("tensor", bank_free[2][0], bank_free[2][1])
                pv = op("tensor", lambda e: e.transpose(
                    ps[2][0:2 * NS, 0:P], bc_t_f[:], ident[:]), "pe")
                wait("scalar", "pe", pv)
                bank_free[2] = _evs(op("scalar", lambda e, ts=ts: e.activation(
                    out=bc_nt[:, ts * P:(ts + 1) * P], in_=ps[2][0:2 * NS, 0:P],
                    func=AF.Identity), "ev"))
            wait("gpsimd", "ev", n["ev"])
            wait("gpsimd", "qb", n["qb"])
            dma("gpsimd", "qb", sc_bc[:], bc_nt[:])
            wait("gpsimd", "qb", n["qb"])
            BCDST = [Brep, Crep]
            wait("tensor", "sc", ev_ones)
            for g in range(8):
                dma("gpsimd", "qb", brow[:, 0:4 * T],
                    sc_bc[:].rearrange("a b -> (a b)")[None, g * 4 * T:(g + 1) * 4 * T])
                wait("tensor", "qb", n["qb"])
                for j in range(4):
                    nidx = (g * 4 + j) % NS
                    dst = BCDST[(g * 4 + j) // NS]
                    psb = 4 + (j % 2)
                    wait("tensor", bank_free[psb][0], bank_free[psb][1])
                    pv = op("tensor", lambda e, j=j, psb=psb: e.matmul(
                        ps[psb][:], ones1[0:1, :], brow[:, j * T:(j + 1) * T],
                        start=True, stop=True), "pe")
                    wait("scalar", "pe", pv)
                    bank_free[psb] = _evs(op("scalar", lambda e, nidx=nidx, dst=dst,
                                             psb=psb: e.activation(
                        out=dst[:, nidx * T:(nidx + 1) * T],
                        in_=ps[psb][:], func=AF.Identity), "ev"))
                if g < 7:
                    wait("gpsimd", "pe", n["pe"])

        def build_ffn_units(mt):
            sq, tok0 = seq_of(mt)
            units = []
            gu_ev = {}

            pend = {}

            def gu_issue(th, kh):
                parw = (th * KH + kh) % 2
                rot = kh % 2
                bg, bu = (0, 1) if rot == 0 else (2, 3)
                qsw = f"qw{parw}"
                qsd = f"qd{parw}"
                wait("sync", "pe", guard_w[parw])
                vg = dma("sync", qsw, wbuf[:, parw], sc_g[kh])
                wait("sync", "pe", guard_d[parw])
                vu = dma("sync", qsd, dbuf[:, parw], sc_u[kh])
                wait("tensor", qsw, vg)
                gev = mm_group(bg, [
                    (lambda kd=kd, parw=parw: wbuf[:, parw, kd * P:(kd + 1) * P],
                     lambda kd=kd, th=th: x1_fm[:, kd * T + th * TH: kd * T + (th + 1) * TH],
                     lambda bg=bg: ps[bg][:, 0:TH]) for kd in range(KD)])
                guard_w[parw] = gev
                wait("tensor", qsd, vu)
                uev = mm_group(bu, [
                    (lambda kd=kd, parw=parw: dbuf[:, parw, kd * P:(kd + 1) * P],
                     lambda kd=kd, th=th: x1_fm[:, kd * T + th * TH: kd * T + (th + 1) * TH],
                     lambda bu=bu: ps[bu][:, 0:TH]) for kd in range(KD)])
                guard_d[parw] = uev
                st_["x1_use_pe"] = uev
                pend[(th, kh)] = (bg, bu, gev, uev)

            def gu_drain(th, kh):
                bg, bu, gev, uev = pend.pop((th, kh))
                par = kh % 2
                gb = tap[:, (2 + par) * TH:(3 + par) * TH]
                ub = tap[:, (4 + par) * TH:(5 + par) * TH]
                tb = tap[:, (6 + par) * TH:(7 + par) * TH]
                wait("scalar", "pe", gev)
                if st_["prev_gu"]:
                    wait("scalar", "dv", st_["prev_gu"])
                a_t = op("scalar", lambda e, bg=bg, tb=tb, kh=kh: e.activation(
                    out=tb, in_=ps[bg][:, 0:TH], func=AF.Tanh,
                    scale=sgh_sb[:, kh:kh + 1]), "ac")
                eg = op("scalar", lambda e, bg=bg, gb=gb, kh=kh: e.activation(
                    out=gb, in_=ps[bg][:, 0:TH], func=AF.Identity,
                    scale=sgh_sb[:, kh:kh + 1]), "ac")
                bank_free[bg] = ("ac", eg)
                wait("scalar", "pe", uev)
                eu = op("scalar", lambda e, bu=bu, ub=ub, kh=kh: e.activation(
                    out=ub, in_=ps[bu][:, 0:TH], func=AF.Identity,
                    scale=su_sb[:, kh:kh + 1]), "ac")
                bank_free[bu] = ("ac", eu)
                wait("vector", "ac", eu)
                m1 = op("vector", lambda e, gb=gb, ub=ub: e.tensor_tensor(
                    out=ub, in0=ub, in1=gb, op=OP.mult), "vv")
                wait("vector", "vv", m1)
                g_ = op("vector", lambda e, kh=kh, tb=tb, ub=ub: e.scalar_tensor_tensor(
                    out=gu_half[:, kh * TH:(kh + 1) * TH], in0=tb, scalar=1.0, in1=ub,
                    op0=OP.add, op1=OP.mult), "dv")
                st_["prev_gu"] = g_
                gu_ev[(th, kh)] = g_

            def down_unit(th, kh):
                parw = kh % 2
                qsd = f"qd{parw}"
                wait("sync", "pe", guard_d[parw])
                vw = dma("sync", qsd, dbuf[:, parw], sc_d[kh])
                wait("tensor", qsd, vw)
                wait("tensor", "dv", gu_ev[(th, kh)])
                for j in range(2):
                    for dg in range(4):
                        psb = j * 4 + dg
                        stt, sp = (kh == 0), (kh == KH - 1)
                        if stt:
                            wait("tensor", bank_free[psb][0], bank_free[psb][1])
                        f = (lambda e, kh=kh, parw=parw, j=j, dg=dg, psb=psb,
                             stt=stt, sp=sp:
                             e.matmul(ps[psb][:],
                                      gu_half[:, kh * TH + j * P: kh * TH + (j + 1) * P],
                                      dbuf[:, parw, dg * 512:(dg + 1) * 512],
                                      start=stt, stop=sp))
                        if j == 1 and dg == 3:
                            op("tensor", f, "pe")
                        else:
                            op("tensor", f)
                guard_d[parw] = n["pe"]
                st_["down_pe"] = n["pe"]

            def ln2_unit(th, jt):
                tsl = 2 * th + jt
                DOWN_PE = st_["down_pe"]
                wait("gpsimd", "dv", n["dv"])
                wait("gpsimd", "qx", n["qx"])
                wait("gpsimd", "pe", DOWN_PE)
                va = dma("gpsimd", "qx", z_ln2[:], sc_x1[tsl])
                wait("vector", "qx", va)
                wait("vector", "pe", DOWN_PE)
                for dg in range(4):
                    psb = jt * 4 + dg
                    bank_free[psb] = _vvs(op("vector", lambda e, psb=psb, dg=dg:
                                             e.tensor_tensor(
                        out=z_ln2[:, dg * 512:(dg + 1) * 512],
                        in0=z_ln2[:, dg * 512:(dg + 1) * 512],
                        in1=ps[psb][:], op=OP.add), "vv"))
                wait("scalar", bank_free[jt * 4 + 3][0], bank_free[jt * 4 + 3][1])
                wait("scalar", "qo", n["qo"])
                ea0 = op("scalar", lambda e: e.activation(
                    out=rn_tm[:], in_=z_ln2[:], func=AF.Identity,
                    accum_out=cols[:, 3:4]), "ac")
                wait("scalar", "ac", ea0)
                ea = op("scalar", lambda e: e.activation(
                    out=rn_tm[:], in_=z_ln2[:], func=AF.Square,
                    accum_out=cols[:, 5:6]), "ac")
                wait("vector", "ac", ea)
                edv = op("vector", lambda e: e.tensor_scalar(
                    out=cols[:, 4:5], in0=cols[:, 3:4], scalar1=1.0 / D,
                    scalar2=None, op0=OP.mult), "dv")
                op("vector", lambda e: e.tensor_scalar(
                    out=cols[:, 6:7], in0=cols[:, 5:6], scalar1=1.0 / D,
                    scalar2=None, op0=OP.mult))
                wait("vector", "dv", edv)
                c7 = op("vector", lambda e: e.tensor_scalar(
                    out=cols[:, 7:8], in0=cols[:, 4:5], scalar1=cols[:, 4:5],
                    scalar2=None, op0=OP.mult), "dv")
                wait("vector", "dv", c7)
                ev2 = op("vector", lambda e: e.tensor_tensor(
                    out=cols[:, 8:9], in0=cols[:, 6:7], in1=cols[:, 7:8],
                    op=OP.subtract), "dv")
                wait("scalar", "dv", ev2)
                a9 = op("scalar", lambda e: e.activation(
                    out=cols[:, 9:10], in_=cols[:, 8:9], func=AF.Ln,
                    bias=cst[:, 1:2]), "ac")
                wait("scalar", "ac", a9)
                er = op("scalar", lambda e: e.activation(
                    out=cols[:, 10:11], in_=cols[:, 9:10], func=AF.Exp,
                    scale=cst[:, 2:3]), "ac")
                wait("vector", "ac", er)
                c11 = op("vector", lambda e: e.tensor_scalar(
                    out=cols[:, 11:12], in0=cols[:, 4:5], scalar1=cols[:, 10:11],
                    scalar2=-1.0, op0=OP.mult, op1=OP.mult), "dv")
                wait("vector", "dv", c11)
                e5 = op("vector", lambda e: e.tensor_scalar(
                    out=rn_tm[:], in0=z_ln2[:], scalar1=cols[:, 10:11],
                    scalar2=cols[:, 11:12], op0=OP.mult, op1=OP.add), "dv")
                wait("gpsimd", "dv", e5)
                dma("gpsimd", "qo", out[sq, tok0 + tsl * P: tok0 + (tsl + 1) * P, :],
                    rn_tm[:])

            for th in range(2):
                for kh in range(KH + 1):
                    def stage(th=th, kh=kh):
                        if kh < KH:
                            gu_issue(th, kh)
                        if kh >= 1:
                            gu_drain(th, kh - 1)
                    units.append(stage)
                for kh in range(KH):
                    units.append(lambda th=th, kh=kh: down_unit(th, kh))
                units.append(lambda th=th: ln2_unit(th, 0))
                units.append(lambda th=th: ln2_unit(th, 1))
            return units

        h3 = h_all.rearrange("p (n t) -> p n t", n=NS)
        b3 = Brep.rearrange("p (n t) -> p n t", n=NS)

        def emit_scan(mt, units, reset_ev):
            cur_scope[0] = f"m{mt}_ph6"
            wait("vector", "ev", n["ev"])
            wait("vector", "dv", ev_sgh)
            wait("scalar", "dv", ev_sgh)
            if reset_ev:
                wait("vector", "sc", reset_ev)
                wait("scalar", "sc", reset_ev)
            ui = 0
            nu = len(units)
            for c in range(NE):
                vdx = op("vector", lambda e, c=c: e.tensor_tensor(
                    out=tap[:, 0:T], in0=dy_fm[:, c * T:(c + 1) * T],
                    in1=x_fm[:, c * T:(c + 1) * T], op=OP.mult), "dv")
                wait("vector", "dv", vdx)
                dxb = tap[:, 0:T].unsqueeze(1).broadcast_to([P, NS, T])
                i_ev = op("vector", lambda e, dxb=dxb: e.tensor_tensor(
                    out=h3, in0=dxb, in1=b3, op=OP.mult), "dv")
                for nn_ in range(NS):
                    kkg = len(scan_ev)
                    par = kkg % 2
                    if kkg >= 2:
                        wait("scalar", "dv", scan_ev[kkg - 2])
                    ad_v = op("scalar", lambda e, c=c, nn_=nn_, par=par: e.activation(
                        out=d_buf[:, par * T:(par + 1) * T],
                        in_=dy_fm[:, c * T:(c + 1) * T],
                        func=AF.Exp, scale=A_sb[:, c * NS + nn_:c * NS + nn_ + 1]), "ad")
                    wait("vector", "dv", i_ev if nn_ == 0 else scan_ev[-1])
                    wait("vector", "ad", ad_v)
                    sev = op("vector", lambda e, nn_=nn_, c=c, par=par:
                             e.tensor_tensor_scan(
                        out=h_all[:, nn_ * T:(nn_ + 1) * T],
                        data0=d_buf[:, par * T:(par + 1) * T],
                        data1=h_all[:, nn_ * T:(nn_ + 1) * T],
                        initial=carry[:, c * NS + nn_:c * NS + nn_ + 1],
                        op0=OP.mult, op1=OP.add), "dv")
                    scan_ev.append(sev)
                wait("vector", "dv", scan_ev[-1])
                vcc = op("vector", lambda e, c=c: e.tensor_copy(
                    out=carry[:, c * NS:(c + 1) * NS],
                    in_=h3[:, :, T - 1]), "dv")
                wait("vector", "dv", vcc)
                vt0 = op("vector", lambda e: e.tensor_tensor(
                    out=h_all[:], in0=h_all[:], in1=Crep[:], op=OP.mult), "dv")
                wait("vector", "dv", vt0)
                vt1 = op("vector", lambda e: e.tensor_tensor(
                    out=h_all[:, 0:8 * T], in0=h_all[:, 0:8 * T],
                    in1=h_all[:, 8 * T:16 * T], op=OP.add), "dv")
                wait("vector", "dv", vt1)
                vt2 = op("vector", lambda e: e.tensor_tensor(
                    out=h_all[:, 0:4 * T], in0=h_all[:, 0:4 * T],
                    in1=h_all[:, 4 * T:8 * T], op=OP.add), "dv")
                wait("vector", "dv", vt2)
                vt3 = op("vector", lambda e: e.tensor_tensor(
                    out=h_all[:, 0:2 * T], in0=h_all[:, 0:2 * T],
                    in1=h_all[:, 2 * T:4 * T], op=OP.add), "dv")
                wait("vector", "dv", vt3)
                op("vector", lambda e, c=c: e.tensor_tensor(
                    out=dy_fm[:, c * T:(c + 1) * T], in0=h_all[:, 0:T],
                    in1=h_all[:, T:2 * T], op=OP.add), "dv")
                tgt = (c + 1) * nu // NE
                while ui < tgt:
                    units[ui]()
                    ui += 1
            return n["dv"]

        def emit_ph7(mt, SCAN_DV):
            cur_scope[0] = f"m{mt}_ph7"
            ym_ev = {}
            if mt == 0:
                wait("sync", "qw0", n["qw0"])
                wait("sync", "qw1", n["qw1"])
            guard_p = {0: max(guard_w[0], guard_w[1]), 1: max(guard_d[0], guard_d[1])}
            cur_buf = wbuf
            cur_vw = ("qw0", 0)
            cur_pi = 0
            for c in range(NE):
                par = c % 2
                sskip = tap[:, (1 + par) * T:(2 + par) * T]
                ytmp = tap[:, 0:T] if par == 0 else tap[:, 3 * T:4 * T]
                if par == 0:
                    pi = (c // 2) % 2
                    buf = wbuf if pi == 0 else dbuf
                    qsem = "qw0" if pi == 0 else "qd0"
                    wait("sync", "pe", guard_p[pi])
                    vw = dma("sync", qsem, buf[:],
                             sc_skip[c:c + 2].rearrange("c p d -> p c d"))
                    cur_vw = (qsem, vw)
                    cur_buf = buf
                    cur_pi = pi
                    wait("tensor", qsem, vw)
                psb = par
                pev = mm_group(psb, [
                    (lambda kd=kd, par=par, cur_buf=cur_buf:
                     cur_buf[:, par, kd * P:(kd + 1) * P],
                     lambda kd=kd: rn_fm[:, kd * T:(kd + 1) * T],
                     lambda psb=psb: ps[psb][:]) for kd in range(KD)],
                    extra_wait=("dv", SCAN_DV) if c < 2 else None)
                guard_p[cur_pi] = pev
                wait("scalar", "pe", pev)
                if c >= 2:
                    wait("scalar", "dv", ym_ev[c - 2])
                esk = op("scalar", lambda e, psb=psb, sskip=sskip: e.activation(
                    out=sskip, in_=ps[psb][:], func=AF.Silu), "ev")
                bank_free[psb] = _evs(esk)
                wait("vector", "ev", esk)
                y1 = op("vector", lambda e, c=c, ytmp=ytmp: e.tensor_scalar(
                    out=ytmp, in0=x_fm[:, c * T:(c + 1) * T],
                    scalar1=WD_sb[:, c:c + 1], scalar2=None, op0=OP.mult), "dv")
                wait("vector", "dv", y1)
                y2 = op("vector", lambda e, c=c, ytmp=ytmp: e.tensor_tensor(
                    out=ytmp, in0=ytmp, in1=dy_fm[:, c * T:(c + 1) * T], op=OP.add), "dv")
                wait("vector", "dv", y2)
                ym_ev[c] = op("vector", lambda e, c=c, ytmp=ytmp, sskip=sskip:
                              e.tensor_tensor(
                    out=dy_fm[:, c * T:(c + 1) * T], in0=ytmp, in1=sskip,
                    op=OP.mult), "dv")
            guard_w[0] = guard_w[1] = guard_p[0]
            guard_d[0] = guard_d[1] = guard_p[1]
            return n["dv"]

        def emit_ph8(mt, Y_DV):
            cur_scope[0] = f"m{mt}_ph8"
            sq, tok0 = seq_of(mt)
            if mt == 0:
                wait("sync", "qp", marks["out_qp"])
                wait("sync", "qb", marks["out_qb"])
                wait("scalar", "ac", max(ulast_cast[0], ulast_cast[1]))
            guard_p = {0: max(guard_w[0], guard_w[1]), 1: max(guard_d[0], guard_d[1])}
            for half in range(2):
                for cp in range(0, NE, 2):
                    pi = (half * (NE // 2) + cp // 2) % 2
                    buf = wbuf if pi == 0 else dbuf
                    qsem = "qw0" if pi == 0 else "qd0"
                    wait("sync", "pe", guard_p[pi])
                    vw = dma("sync", qsem, buf[:],
                             sc_out[cp:cp + 2].rearrange("c p d -> p c d"))
                    wait("tensor", qsem, vw)
                    for k in range(2):
                        c = cp + k
                        if half == 0 and c < 2:
                            wait("tensor", "dv", Y_DV)
                        for j in range(8):
                            tsl = 2 * half + j // 4
                            dg = j % 4
                            psb = j
                            stt, sp = (c == 0), (c == NE - 1)
                            if stt:
                                wait("tensor", bank_free[psb][0], bank_free[psb][1])
                            f = (lambda e, c=c, k=k, buf=buf, tsl=tsl, dg=dg, psb=psb,
                                 stt=stt, sp=sp:
                                 e.matmul(ps[psb][:],
                                          dy_fm[:, c * T + tsl * P: c * T + (tsl + 1) * P],
                                          buf[:, k, dg * 512:(dg + 1) * 512],
                                          start=stt, stop=sp))
                            if j == 7:
                                op("tensor", f, "pe")
                            else:
                                op("tensor", f)
                    guard_p[pi] = n["pe"]
                OUT_PE = n["pe"]
                for jt in range(2):
                    tsl = 2 * half + jt
                    wait("gpsimd", "dv", n["dv"])
                    va = dma("gpsimd", "qz", z_x[:],
                             resid[sq, tok0 + tsl * P: tok0 + (tsl + 1) * P, :])
                    wait("vector", "qz", va)
                    wait("vector", "pe", OUT_PE)
                    z0 = op("vector", lambda e: e.tensor_scalar(
                        out=z_x[:], in0=z_x[:], scalar1=2.0, scalar2=None,
                        op0=OP.mult), "dv")
                    wait("vector", "dv", z0)
                    for dg in range(4):
                        psb = jt * 4 + dg
                        bank_free[psb] = _vvs(op("vector", lambda e, psb=psb, dg=dg:
                                                 e.tensor_tensor(
                            out=z_x[:, dg * 512:(dg + 1) * 512],
                            in0=z_x[:, dg * 512:(dg + 1) * 512],
                            in1=ps[psb][:], op=OP.add), "vv"))
                    wait("scalar", bank_free[jt * 4 + 3][0], bank_free[jt * 4 + 3][1])
                    wait("scalar", "qx", n["qx"])
                    wait("scalar", "qo", n["qo"])
                    wait("scalar", "pe", st_["lastdma_rn_pe"])
                    ea0 = op("scalar", lambda e: e.activation(
                        out=rn_tm[:], in_=z_x[:], func=AF.Identity,
                        accum_out=cols[:, 3:4]), "ac")
                    wait("scalar", "ac", ea0)
                    ea = op("scalar", lambda e: e.activation(
                        out=rn_tm[:], in_=z_x[:], func=AF.Square,
                        accum_out=cols[:, 5:6]), "ac")
                    wait("vector", "ac", ea)
                    edv = op("vector", lambda e: e.tensor_scalar(
                        out=cols[:, 4:5], in0=cols[:, 3:4], scalar1=1.0 / D,
                        scalar2=None, op0=OP.mult), "dv")
                    op("vector", lambda e: e.tensor_scalar(
                        out=cols[:, 6:7], in0=cols[:, 5:6], scalar1=1.0 / D,
                        scalar2=None, op0=OP.mult))
                    wait("vector", "dv", edv)
                    c7 = op("vector", lambda e: e.tensor_scalar(
                        out=cols[:, 7:8], in0=cols[:, 4:5], scalar1=cols[:, 4:5],
                        scalar2=None, op0=OP.mult), "dv")
                    wait("vector", "dv", c7)
                    ev2 = op("vector", lambda e: e.tensor_tensor(
                        out=cols[:, 8:9], in0=cols[:, 6:7], in1=cols[:, 7:8],
                        op=OP.subtract), "dv")
                    wait("scalar", "dv", ev2)
                    a9 = op("scalar", lambda e: e.activation(
                        out=cols[:, 9:10], in_=cols[:, 8:9], func=AF.Ln,
                        bias=cst[:, 1:2]), "ac")
                    wait("scalar", "ac", a9)
                    er = op("scalar", lambda e: e.activation(
                        out=cols[:, 10:11], in_=cols[:, 9:10], func=AF.Exp,
                        scale=cst[:, 2:3]), "ac")
                    wait("vector", "ac", er)
                    c11 = op("vector", lambda e: e.tensor_scalar(
                        out=cols[:, 11:12], in0=cols[:, 4:5], scalar1=cols[:, 10:11],
                        scalar2=-1.0, op0=OP.mult, op1=OP.mult), "dv")
                    wait("vector", "dv", c11)
                    e4 = op("vector", lambda e: e.tensor_scalar(
                        out=rn_tm[:], in0=z_x[:], scalar1=cols[:, 10:11],
                        scalar2=cols[:, 11:12], op0=OP.mult, op1=OP.add), "dv")
                    wait("gpsimd", "dv", e4)
                    wait("gpsimd", "qx", n["qx"])
                    dma("gpsimd", "qx", sc_x1[tsl], rn_tm[:])
                    wait("tensor", "dv", e4)
                    if half == 0 and jt == 0:
                        wait("tensor", "pe", st_["x1_use_pe"])
                    for kd in range(KD):
                        psb = kd % 2
                        wait("tensor", bank_free[psb][0], bank_free[psb][1])
                        pv = op("tensor", lambda e, kd=kd, psb=psb: e.transpose(
                            ps[psb][:, 0:P], rn_tm[:, kd * P:(kd + 1) * P], ident[:]), "pe")
                        wait("scalar", "pe", pv)
                        bank_free[psb] = _evs(op("scalar", lambda e, kd=kd, tsl=tsl,
                                                 psb=psb: e.activation(
                            out=x1_fm[:, kd * T + tsl * P: kd * T + (tsl + 1) * P],
                            in_=ps[psb][:, 0:P], func=AF.Identity), "ev"))
                    st_["lastdma_rn_pe"] = n["pe"]
            guard_w[0] = guard_w[1] = guard_p[0]
            guard_d[0] = guard_d[1] = guard_p[1]

        rn_ev0 = emit_ph1(0)
        emit_ph2(0, ev_zero, rn_ev0)
        emit_ph5(0)
        emit_ph34(0)
        prev_units = build_prep_units()
        pending_reset = ev_zero
        for mt in range(MT):
            if mt == 1:
                wait("sync", "qp", marks["ffn_qp"])
                wait("sync", "qb", marks["ffn_qb"])
                wait("sync", "qw0", n["qw0"])
                wait("sync", "qw1", n["qw1"])
            SCAN_DV = emit_scan(mt, prev_units, pending_reset)
            Y_DV = emit_ph7(mt, SCAN_DV)
            emit_ph8(mt, Y_DV)
            prev_units = build_ffn_units(mt)
            if mt + 1 < MT:
                nxt = mt + 1
                if nxt % (L // T) == 0:
                    wait("gpsimd", "dv", n["dv"])
                    op("gpsimd", lambda e: e.memset(carry[:], 0.0))
                    reset_ev = op("gpsimd", lambda e: e.memset(tails[:], 0.0), "sc")
                else:
                    reset_ev = None
                pending_reset = reset_ev
                rn_evn = emit_ph1(nxt)
                emit_ph2(nxt, reset_ev, rn_evn)
                emit_ph5(nxt)
                emit_ph34(nxt)
        cur_scope[0] = "tail_ffn"
        for u in prev_units:
            u()

        wait("gpsimd", "qo", n["qo"])
        wait("sync", "qo", n["qo"])

        from contextlib import nullcontext

        def register(engname):
            def body(eng):
                for scope, f in plan[engname]:
                    cm = nc.named_scope(scope) if scope else nullcontext()
                    with cm:
                        f(eng)
            return body

        block.sync(register("sync"))
        block.gpsimd(register("gpsimd"))
        block.tensor(register("tensor"))
        block.vector(register("vector"))
        block.scalar(register("scalar"))

    return nc


def kernel(**inputs):
    h = _prep_inputs(inputs)
    resid = np.ascontiguousarray(np.asarray(inputs["L0_attn_resid"], dtype=np.float32))
    nc = build()
    in_maps = []
    for core in range(NCORES):
        in_maps.append({
            "resid": resid[core * SEQS:(core + 1) * SEQS],
            "inwT": h["inwT"], "skipwT": h["skipwT"], "outwT": h["outwT"],
            "wd1T": h["wd1T"], "wd2T": h["wd2T"], "wBT": h["wBT"], "wCT": h["wCT"],
            "gwT": h["gwT"], "uwT": h["uwT"], "dwT": h["dwT"],
            "sgate_c": h["sgate_c"], "sup_c": h["sup_c"], "s_down": h["s_down"],
            "normw": h["normw"], "convw": h["convw"], "convb": h["convb"],
            "wd2b": h["wd2b"], "WD": h["WD"], "Alog": h["Alog"],
        })
    res = run_bass_kernel_spmd(nc, in_maps, core_ids=list(range(NCORES)))
    outs = [np.asarray(r["out"]) for r in res.results]
    return np.concatenate(outs, axis=0).reshape(16, L, D).astype(np.float32)
